# version 20
# baseline (speedup 1.0000x reference)
import numpy as np
import ml_dtypes
from contextlib import ExitStack

import concourse.bass as bass
import concourse.mybir as mybir
from concourse.bass_utils import run_bass_kernel_spmd

F32 = mybir.dt.float32
BF16 = mybir.dt.bfloat16
I32 = mybir.dt.int32
AF = mybir.ActivationFunctionType
ALU = mybir.AluOpType
AX = mybir.AxisListType

T = 2048
D = 1024
NT = T // 128
ND = D // 128
CH = 128
NCH = T // CH
HID = 3072
NSLAB = HID // 128
EXP = 8
CAP = 2048
EPS = 1e-5

ENGS = ("pe", "act", "dve", "pool", "sp")
SAME_ENGINE_SYNC = True
QUEUES = ("qsp", "qact", "qpool", "qcc")
Q_ISSUER = {"qsp": "sp", "qact": "act", "qpool": "pool", "qcc": "pool"}
UNIT = dict({e: 1 for e in ENGS}, qsp=16, qact=16, qpool=16, qcc=1)


class Cmd:
    __slots__ = ("key", "idx", "build", "waits")

    def __init__(self, key, idx, build):
        self.key, self.idx, self.build, self.waits = key, idx, build, []


class Prog:

    NLANE = 6

    def __init__(self, nc, stack, same_engine_sync=True):
        self.nc = nc
        self.same_engine_sync = same_engine_sync
        self.tick = {k: stack.enter_context(nc.semaphore(f"tk_{k}"))
                     for k in ENGS}
        self.qtick = {}
        for q in QUEUES:
            nl = 1 if q == "qcc" else self.NLANE
            self.qtick[q] = [stack.enter_context(nc.semaphore(f"tk_{q}_{l}"))
                             for l in range(nl)]
        self.cmds = {e: [] for e in ENGS}
        self.n = {k: 0 for k in (*ENGS, *QUEUES)}
        self.marks = {k: set() for k in ENGS}
        self.last_wait = {e: {} for e in ENGS}
        self.res = {}

    def _qwait(self, cmd, waiter, queue, lane, val):
        k = ("q", queue, lane)
        if self.last_wait[waiter].get(k, -1) >= val:
            return
        self.last_wait[waiter][k] = val
        cmd.waits.append(("q", queue, lane, val))

    def _ewait(self, cmd, waiter, eng, idx):
        if eng == waiter:
            if waiter == "pe" or not self.same_engine_sync:
                return
        cand, limit = None, self.n[eng]
        for m in self.marks[eng]:
            if idx <= m < limit and (cand is None or m < cand):
                cand = m
        if cand is None:
            self.marks[eng].add(idx)
            cand = idx
        if self.last_wait[waiter].get(eng, -1) >= cand:
            return
        self.last_wait[waiter][eng] = cand
        cmd.waits.append(("e", eng, cand))

    def _wait(self, cmd, waiter, prod):
        pkey, pidx = prod
        if pkey in ENGS:
            self._ewait(cmd, waiter, pkey, pidx)
        else:
            nl = 1 if pkey == "qcc" else self.NLANE
            lane, ordinal = pidx % nl, pidx // nl
            unit = 1 if pkey == "qcc" else 16
            self._qwait(cmd, waiter, pkey, lane, (ordinal + 1) * unit)

    def _deps(self, cmd, waiter, reads, writes):
        for r in reads:
            e = self.res.get(r)
            if e and e["w"] is not None:
                self._wait(cmd, waiter, e["w"])
        for w in writes:
            e = self.res.get(w)
            if e:
                if e["w"] is not None:
                    self._wait(cmd, waiter, e["w"])
                for rr in e["r"]:
                    self._wait(cmd, waiter, rr)

    def _record(self, key, idx, reads, writes):
        for r in reads:
            self.res.setdefault(r, {"w": None, "r": []})["r"].append((key, idx))
        for w in writes:
            e = self.res.setdefault(w, {"w": None, "r": []})
            e["w"] = (key, idx)
            e["r"] = []

    def op(self, eng, build, reads=(), writes=()):
        idx = self.n[eng]
        cmd = Cmd(eng, idx, build)
        self._deps(cmd, eng, reads, writes)
        self.cmds[eng].append(cmd)
        self.n[eng] += 1
        self._record(eng, idx, reads, writes)
        return (eng, idx)

    def dma(self, queue, out, in_, reads=(), writes=()):
        issuer = Q_ISSUER[queue]
        qidx = self.n[queue]
        lane, ordinal = qidx % self.NLANE, qidx // self.NLANE

        def build(e, out=out, in_=in_):
            return e.dma_start(out, in_)
        cmd = Cmd(queue, qidx, build)
        cmd.idx = (lane, ordinal)
        self._deps(cmd, issuer, reads, writes)
        if ordinal > 0:
            self._qwait(cmd, issuer, queue, lane, ordinal * 16)
        self.cmds[issuer].append(cmd)
        self.n[queue] += 1
        self._record(queue, qidx, reads, writes)
        return (queue, qidx)

    def cc(self, build, reads=(), writes=()):
        qidx = self.n["qcc"]
        cmd = Cmd("qcc", qidx, build)
        cmd.idx = (0, qidx)
        self._deps(cmd, "pool", reads, writes)
        if qidx > 0:
            self._qwait(cmd, "pool", "qcc", 0, qidx)
        self.cmds["pool"].append(cmd)
        self.n["qcc"] += 1
        self._record("qcc", qidx, reads, writes)
        return ("qcc", qidx)

    def barrier(self, engines=ENGS):
        etargets = [(k, self.n[k] - 1) for k in ENGS if self.n[k] > 0]
        for e in engines:
            cmd = Cmd(e, None, None)
            for (k, i) in etargets:
                if k == e:
                    continue
                self._ewait(cmd, e, k, i)
            for q in QUEUES:
                nl = 1 if q == "qcc" else self.NLANE
                unit = 1 if q == "qcc" else 16
                for lane in range(nl):
                    cnt = (self.n[q] - lane + nl - 1) // nl
                    if cnt > 0:
                        self._qwait(cmd, e, q, lane, cnt * unit)
            if cmd.waits:
                self.cmds[e].append(cmd)

    def finish(self, block):
        rank = {}
        for k in ENGS:
            for i, m in enumerate(sorted(self.marks[k])):
                rank[(k, m)] = i + 1

        def runner(ename):
            def body(engine):
                for cmd in self.cmds[ename]:
                    for w in cmd.waits:
                        if w[0] == "e":
                            engine.wait_ge(self.tick[w[1]], rank[(w[1], w[2])])
                        else:
                            engine.wait_ge(self.qtick[w[1]][w[2]], w[3])
                    if cmd.build is None:
                        continue
                    inst = cmd.build(engine)
                    if cmd.key in ENGS:
                        if cmd.idx is not None and cmd.idx in self.marks[cmd.key]:
                            if inst is None:
                                inst = engine.engine_nop()
                            inst.then_inc(self.tick[cmd.key], 1)
                    else:
                        lane, _ = cmd.idx
                        unit = 1 if cmd.key == "qcc" else 16
                        inst.then_inc(self.qtick[cmd.key][lane], unit)
            return body
        block.tensor(runner("pe"))
        block.scalar(runner("act"))
        block.vector(runner("dve"))
        block.gpsimd(runner("pool"))
        block.sync(runner("sp"))



def _host_constants(core):
    c = {}
    u = np.triu(np.ones((CH, CH), np.float32))
    c["u_aug"] = np.ascontiguousarray(
        np.concatenate([u, np.ones((CH, 1), np.float32)], 1))
    c["l_strict"] = np.ascontiguousarray(
        np.tril(np.ones((128, 128), np.float32), -1))
    c["ones_f"] = np.ones((128, 128), np.float32)
    c["ident_f"] = np.eye(128, dtype=np.float32)
    c["ident_b"] = np.eye(128, dtype=ml_dtypes.bfloat16)
    dh = 64
    inv = 1.0 / (10000.0 ** (np.arange(0, dh, 2, dtype=np.float32) / dh))
    fr = np.outer(np.arange(T, dtype=np.float32), inv)
    emb = np.concatenate([fr, fr], -1)
    cosf = np.cos(emb).T.astype(np.float32)
    sinf = np.sin(emb).T.astype(np.float32)
    c["cos_f"] = np.ascontiguousarray(
        np.concatenate([cosf, cosf], 0).astype(ml_dtypes.bfloat16))
    c["sin_f"] = np.ascontiguousarray(
        np.concatenate([sinf, sinf], 0).astype(ml_dtypes.bfloat16))
    oh = np.zeros((1, 8), np.float32)
    oh[0, core] = 1.0
    c["onehot"] = oh
    mlt = np.zeros((8, 1), np.float32)
    mlt[:core, 0] = 1.0
    c["masklt_rep"] = np.ascontiguousarray(np.broadcast_to(mlt, (8, 128)))
    pm = np.zeros((128, 128), np.float32)
    for o in (0, 64):
        for r in range(32):
            pm[o + 32 + r, o + r] = -1.0
            pm[o + r, o + 32 + r] = 1.0
    c["permr"] = pm.astype(ml_dtypes.bfloat16)
    hs = np.zeros((16, 1024), np.float32)
    for p in range(8):
        hs[2 * p, p * 128:p * 128 + 64] = 1.0
        hs[2 * p + 1, p * 128 + 64:(p + 1) * 128] = 1.0
    c["hselb"] = hs.astype(ml_dtypes.bfloat16)
    return c


def make_in_maps(inputs):
    x = np.asarray(inputs["L0_attn_x"])
    wgk2b = np.concatenate([np.asarray(inputs["L0_attn_Wgk2"]),
                            np.asarray(inputs["L0_attn_bgk2"])[None, :]], 0)
    rw = np.asarray(inputs["L0_attn_rms_w"], dtype=np.float32)
    rmsw_f = np.ascontiguousarray(rw.reshape(2, 128).T)
    shared = {
        "wq": np.ascontiguousarray(inputs["L0_attn_Wq"]),
        "wk": np.ascontiguousarray(inputs["L0_attn_Wk"]),
        "wv": np.ascontiguousarray(inputs["L0_attn_Wv"]),
        "wg0": np.ascontiguousarray(inputs["L0_attn_Wg"]),
        "wgk1": np.ascontiguousarray(inputs["L0_attn_Wgk1"]),
        "wgk2b": np.ascontiguousarray(wgk2b),
        "wout": np.ascontiguousarray(inputs["L0_attn_Wout"]),
        "rmsw_f": rmsw_f,
        "wgt": np.ascontiguousarray(np.asarray(inputs["L0_ffn_w_gate"]).T),
        "wut": np.ascontiguousarray(np.asarray(inputs["L0_ffn_w_up"]).T),
        "wdt": np.ascontiguousarray(np.asarray(inputs["L0_ffn_w_down"]).T),
        "wqkv": np.ascontiguousarray(inputs["L1_attn_Wqkv"]),
        "wo": np.ascontiguousarray(inputs["L1_attn_Wo"]),
        "wgm": np.ascontiguousarray(inputs["L1_ffn_wg"]),
        "wet": np.ascontiguousarray(
            np.transpose(np.asarray(inputs["L1_ffn_W"]), (0, 2, 1))),
        "bm": np.ascontiguousarray(inputs["L1_ffn_b"]),
    }
    maps = []
    for core in range(8):
        m = dict(shared)
        m["x"] = np.ascontiguousarray(np.asarray(x[core], dtype=np.float32))
        m.update(_host_constants(core))
        maps.append(m)
    return maps


def col_slab(w, f0, fw):
    return w[:, f0:f0 + fw].rearrange("(kc p) f -> p kc f", p=128)



def build_program(taps=(), stages=("l0a", "l0f", "l1a", "l1f")):
    nc = bass.Bass("TRN2", target_bir_lowering=False)
    g = {}

    def din(name, shape, dtype=F32):
        g[name] = nc.dram_tensor(name, list(shape), dtype,
                                 kind="ExternalInput").ap()

    din("x", [T, D])
    din("wq", [D, 512]); din("wk", [D, 512]); din("wv", [D, D]); din("wg0", [D, D])
    din("wgk1", [D, 16]); din("wgk2b", [17, 512]); din("wout", [D, D])
    din("rmsw_f", [128, 2])
    din("wgt", [D, HID], I32); din("wut", [D, HID], I32); din("wdt", [HID, D], I32)
    din("wqkv", [D, 3 * D]); din("wo", [D, D])
    din("wgm", [D, EXP]); din("wet", [EXP, D, D]); din("bm", [EXP, D])
    din("u_aug", [CH, CH + 1]); din("l_strict", [128, 128]); din("ones_f", [128, 128])
    din("ident_f", [128, 128]); din("ident_b", [128, 128], BF16)
    din("cos_f", [128, T], BF16); din("sin_f", [128, T], BF16)
    din("onehot", [1, 8]); din("masklt_rep", [8, 128])
    din("permr", [128, 128], BF16); din("hselb", [16, 1024], BF16)

    out = nc.dram_tensor("out", [T, D], F32, kind="ExternalOutput").ap()
    tap_outs = {}
    for tname in taps:
        tap_outs[tname] = nc.dram_tensor("tap_" + tname, [T, D], F32,
                                         kind="ExternalOutput").ap()
    cc_in_t = nc.dram_tensor("cc_in", [8, 8], F32, kind="Internal")
    cc_out_t = nc.dram_tensor("cc_out", [8, 8], F32, kind="Internal")

    stack = ExitStack()
    with stack:
        def sb(name, shape, dt=F32):
            return stack.enter_context(nc.sbuf_tensor(name, shape, dt))

        P = Prog(nc, stack, same_engine_sync=SAME_ENGINE_SYNC)

        RES = sb("RES", [128, NT * D])
        XT = sb("XT", [128, ND * T], BF16)
        WB = sb("WB", [128, 8 * 1024], BF16)
        UAUG = sb("UAUG", [CH, CH + 1])
        LSTRF = sb("LSTRF", [128, 128])
        ONESF = sb("ONESF", [128, 128])
        IDF = sb("IDF", [128, 128])
        IDB = sb("IDB", [128, 128], BF16)
        RMSWF = sb("RMSWF", [128, 2])
        LNT = sb("LNT", [128, 2048])
        GARB = sb("GARB", [128, 512], BF16)
        LNC = sb("LNC", [128, 16])
        LNEPS = sb("LNEPS", [128, 1])
        ONE1 = sb("ONE1", [128, 1])

        def wslot(i):
            return WB[:, i * 1024:(i + 1) * 1024]

        def wtile(i, k, w=128):
            return WB[:, i * 1024 + k * w: i * 1024 + (k + 1) * w]

        def tap(name, reads):
            if name in tap_outs:
                P.dma("qsp",
                      tap_outs[name].rearrange("(i p) d -> p i d", p=128),
                      RES[:].rearrange("p (i d) -> p i d", d=D), reads=reads)

        for nm, dst in (("u_aug", UAUG), ("l_strict", LSTRF), ("ones_f", ONESF),
                        ("ident_f", IDF), ("ident_b", IDB), ("rmsw_f", RMSWF)):
            P.dma("qsp", dst[:], g[nm], writes=[nm])
        P.op("dve", lambda v: v.memset(LNEPS[:], EPS), writes=["lneps"])
        P.op("dve", lambda v: v.memset(ONE1[:], 1.0), writes=["one1"])
        for i in range(NT):
            P.dma("qsp", RES[:, i * D:(i + 1) * D],
                  g["x"][i * 128:(i + 1) * 128, :], writes=[("res", i)])

        res_r = [("res", i) for i in range(NT)]

        def xt_r(i=None):
            if i is None:
                return [("xt", k, j) for k in range(ND) for j in range(NT)]
            return [("xt", k, i) for k in range(ND)]

        def build_xt(tag):
            with ExitStack() as st:
                PT = [st.enter_context(nc.psum_tensor(f"ptx_{tag}{b}", [128, 512]))
                      for b in range(4)]
                idx = 0
                for i in range(NT):
                    for k in range(ND):
                        b = idx % 4
                        idx += 1
                        P.op("pe", lambda t, i=i, k=k, b=b: t.transpose(
                                PT[b][:, 0:128],
                                RES[:, i * D + k * 128: i * D + (k + 1) * 128],
                                IDF[:]),
                             reads=[("res", i), "ident_f"],
                             writes=[("ptx", b)])
                        P.op("dve", lambda v, i=i, k=k, b=b: v.tensor_copy(
                                XT[:, k * T + i * 128: k * T + (i + 1) * 128],
                                PT[b][:, 0:128]),
                             reads=[("ptx", b)], writes=[("xt", k, i)])

        def ln_apply(i, f0, f1, f_reads, keep_col=None, keep_reads=()):
            pr = i % 2
            LT = LNT[:, pr * 1024:(pr + 1) * 1024]
            LC = LNC[:, pr * 8:(pr + 1) * 8]
            def ck(j):
                return ("lnc", pr, j)
            kt0, kt1, ksq = ("lnt0", pr), ("lnt1", pr), ("lnsq", pr)
            kc = 1.0 if keep_col is None else keep_col
            P.op("dve", lambda v: v.scalar_tensor_tensor(
                    LT[:, 0:512], f0, kc, RES[:, i * D: i * D + 512],
                    ALU.mult, ALU.add, accum_out=LC[:, 0:1]),
                 reads=[("res", i), *f_reads, *keep_reads], writes=[kt0, ck(0)])
            P.op("dve", lambda v: v.scalar_tensor_tensor(
                    LT[:, 512:1024], f1, kc, RES[:, i * D + 512:(i + 1) * D],
                    ALU.mult, ALU.add, accum_out=LC[:, 1:2]),
                 reads=[("res", i), *f_reads, *keep_reads], writes=[kt1, ck(1)])
            P.op("dve", lambda v: v.scalar_tensor_tensor(
                    GARB[:], LT[:, 0:512], 1.0, LT[:, 0:512],
                    ALU.mult, ALU.mult, accum_out=LC[:, 2:3]),
                 reads=[kt0], writes=["garb", ck(2)])
            P.op("dve", lambda v: v.scalar_tensor_tensor(
                    GARB[:], LT[:, 512:1024], 1.0, LT[:, 512:1024],
                    ALU.mult, ALU.mult, accum_out=LC[:, 3:4]),
                 reads=[kt1], writes=["garb", ck(3)])
            P.op("dve", lambda v: v.tensor_add(LC[:, 4:5], LC[:, 0:1],
                                               LC[:, 1:2]),
                 reads=[ck(0), ck(1)], writes=[ck(4)])
            P.op("dve", lambda v: v.tensor_add(LC[:, 5:6], LC[:, 2:3],
                                               LC[:, 3:4]),
                 reads=[ck(2), ck(3)], writes=[ck(5)])
            P.op("dve", lambda v: v.tensor_scalar(
                    LC[:, 6:7], LC[:, 4:5], 1.0 / D, None, ALU.mult),
                 reads=[ck(4)], writes=[ck(6)])
            P.op("dve", lambda v: v.scalar_tensor_tensor(
                    LC[:, 7:8], LC[:, 6:7], LC[:, 4:5], LC[:, 5:6],
                    ALU.mult, ALU.subtract),
                 reads=[ck(6), ck(4), ck(5)], writes=[ck(7)])
            P.op("act", lambda s: s.activation(
                    LC[:, 7:8], LC[:, 7:8], AF.Ln, bias=LNEPS[:, 0:1],
                    scale=-1.0 / D),
                 reads=[ck(7), "lneps"], writes=[ck(7)])
            P.op("act", lambda s: s.activation(
                    LC[:, 7:8], LC[:, 7:8], AF.Exp, scale=-0.5),
                 reads=[ck(7)], writes=[ck(7)])
            P.op("dve", lambda v: v.scalar_tensor_tensor(
                    LC[:, 6:7], LC[:, 6:7], -1.0, LC[:, 7:8],
                    ALU.mult, ALU.mult),
                 reads=[ck(6), ck(7)], writes=[ck(6)])
            P.op("dve", lambda v: v.tensor_scalar(
                    RES[:, i * D: i * D + 512], LT[:, 0:512],
                    LC[:, 7:8], LC[:, 6:7], ALU.mult, ALU.add),
                 reads=[kt0, ck(6), ck(7)], writes=[("res", i)])
            P.op("dve", lambda v: v.tensor_scalar(
                    RES[:, i * D + 512:(i + 1) * D], LT[:, 512:1024],
                    LC[:, 7:8], LC[:, 6:7], ALU.mult, ALU.add),
                 reads=[kt1, ck(6), ck(7)], writes=[("res", i)])

        def stage_l0a():
            with ExitStack() as st:
                def lsb(n, s, d=F32):
                    return st.enter_context(nc.sbuf_tensor(n, s, d))

                def lps(n, s, d=F32):
                    return st.enter_context(nc.psum_tensor(n, s, d))
                PS = [lps(f"pa{i}", [128, 512]) for i in range(7)]
                st_scan = ExitStack()
                PSB = st_scan.enter_context(nc.psum_tensor("pab", [128, 1024], BF16))
                UT = lsb("UT", [32, T], BF16)
                GSA = lsb("GSA", [128, 8 * 129])
                GSB = lsb("GSB", [128, 8 * 129])
                AEND = lsb("AEND", [128, 16])
                GH = lsb("GH", [128, 1024])
                QT = lsb("QT", [128, 1024], BF16)
                KT = lsb("KT", [128, 1024], BF16)
                KTOK = lsb("KTOK", [128, 256], BF16)
                KH = lsb("KH", [128, 256], BF16)
                VT = lsb("VT", [128, 512], BF16)
                SST = [lsb("SST0", [128, 256]), lsb("SST1", [128, 256])]
                SSTB = lsb("SSTB", [128, 256], BF16)
                OBUF = lsb("OBUF", [128, 2 * T])
                OSQ = lsb("OSQ", [128, 1024], BF16)
                OSF = lsb("OSF", [128, 1024])
                OST = OSF[0:1, 0:512]
                INVR = lsb("INVR", [1, 512], BF16)
                ONEROWB = lsb("ONEROWB", [1, 128], BF16)
                ONECOLB = lsb("ONECOLB", [128, 1], BF16)
                GATE = lsb("GATE", [128, 1024], BF16)
                OPRIME = lsb("OPRIME", [128, 8 * T], BF16)
                WGK1 = lsb("WGK1", [128, 128], BF16)
                WGK2B = lsb("WGK2B", [17, 512], BF16)
                PTS = lsb("PTS", [128, 128], BF16)

                P.op("dve", lambda v: v.memset(ONEROWB[:], 1.0),
                     writes=["onerowb"])
                P.op("dve", lambda v: v.memset(ONECOLB[:], 1.0),
                     writes=["onecolb"])
                P.dma("qpool", WGK1[:].rearrange("p (kc f) -> p kc f", f=16),
                      col_slab(g["wgk1"], 0, 16), writes=["wgk1"])
                P.dma("qpool", WGK2B[:], g["wgk2b"], writes=["wgk2b"])
                P.op("dve", lambda v: v.memset(UT[:], 1.0), writes=["ut"])
                for tc in range(4):
                    for k in range(ND):
                        P.op("pe", lambda t, tc=tc, k=k: t.matmul(
                                PS[0][0:16, 0:512], WGK1[:, k * 16:(k + 1) * 16],
                                XT[:, k * T + tc * 512: k * T + (tc + 1) * 512],
                                start=(k == 0), stop=(k == ND - 1)),
                             reads=["wgk1", *xt_r()], writes=[("psb", 0)])
                    P.op("dve", lambda v, tc=tc: v.tensor_copy(
                            UT[0:16, tc * 512:(tc + 1) * 512], PS[0][0:16, 0:512]),
                         reads=[("psb", 0)], writes=["ut"])

                def prefetch_head2(h):
                    b = (h % 2) * 4
                    P.dma("qpool", wslot(b + 0).rearrange("p (kc f) -> p kc f", f=128),
                          col_slab(g["wq"], h * 128, 128),
                          writes=[("wb", b + 0)])
                    P.dma("qpool", wslot(b + 1).rearrange("p (kc f) -> p kc f", f=128),
                          col_slab(g["wk"], h * 128, 128),
                          writes=[("wb", b + 1)])
                    P.dma("qpool", WB[:, (b + 2) * 1024:(b + 4) * 1024]
                          .rearrange("p (kc f) -> p kc f", f=256),
                          col_slab(g["wv"], h * 256, 256),
                          writes=[("wb", b + 2), ("wb", b + 3)])

                prefetch_head2(0)
                for h in range(4):
                    b = (h % 2) * 4
                    if h < 3:
                        prefetch_head2(h + 1)
                    P.op("dve", lambda v: v.memset(SST[0][:], 0.0),
                         writes=["sst0"])
                    P.op("dve", lambda v: v.memset(SSTB[:], 0.0),
                         writes=["sstb"])
                    for half in range(2):
                        for cc in range(8):
                            c = half * 8 + cc
                            zb = cc % 2
                            P.op("pe", lambda t, c=c, h=h, zb=zb: t.matmul(
                                    PS[zb][:, 0:128],
                                    UT[0:17, c * 128:(c + 1) * 128],
                                    WGK2B[0:17, h * 128:(h + 1) * 128],
                                    start=True, stop=True),
                                 reads=["ut", "wgk2b"], writes=[("psb", zb)])
                            P.op("dve", lambda v, cc=cc, zb=zb: v.tensor_copy(
                                    GSA[:, cc * 128:(cc + 1) * 128],
                                    PS[zb][:, 0:128]),
                                 reads=[("psb", zb)], writes=["gsa"])
                        ZV = GSA[:, 0:1024]
                        LV = GSB[:, 0:1024]
                        P.op("dve", lambda v: v.tensor_scalar_min(LV, ZV, 0.0),
                             reads=["gsa"], writes=["gsb"])
                        P.op("act", lambda s: s.activation(ZV, ZV, AF.Abs),
                             reads=["gsa"], writes=["gsa"])
                        P.op("act", lambda s: s.activation(ZV, ZV, AF.Exp,
                                                           scale=-1.0),
                             reads=["gsa"], writes=["gsa"])
                        P.op("act", lambda s: s.activation(ZV, ZV, AF.Ln,
                                                           bias=ONE1[:, 0:1]),
                             reads=["gsa", "one1"], writes=["gsa"])
                        P.op("dve", lambda v: v.tensor_sub(GH[:], LV, ZV),
                             reads=["gsa", "gsb"], writes=["gh"])
                        P.op("dve", lambda v: v.tensor_scalar(
                                GH[:], GH[:], 1.0 / 16.0, -3.0, ALU.mult, ALU.max),
                             reads=["gh"], writes=["gh"])
                        for cc in range(8):
                            zb = cc % 2
                            P.op("pe", lambda t, cc=cc, zb=zb: t.matmul(
                                    PS[zb][:, 0:129],
                                    GH[:, cc * 128:(cc + 1) * 128], UAUG[:],
                                    start=True, stop=True),
                                 reads=["gh", "u_aug"], writes=[("psb", zb)])
                            P.op("act", lambda s, cc=cc, zb=zb: s.activation(
                                    GSA[:, cc * 128:(cc + 1) * 128],
                                    PS[zb][:, 0:128], AF.Exp),
                                 reads=[("psb", zb)], writes=["gsa"])
                            P.op("act", lambda s, cc=cc, zb=zb: s.activation(
                                    AEND[:, cc:cc + 1],
                                    PS[zb][:, 128:129], AF.Exp),
                                 reads=[("psb", zb)], writes=["aend"])
                            P.op("act", lambda s, cc=cc, zb=zb: s.activation(
                                    GSB[:, cc * 128:(cc + 1) * 128],
                                    PS[zb][:, 0:128], AF.Exp, scale=-1.0),
                                 reads=[("psb", zb)], writes=["gsb"])
                        for tq in range(2):
                            tc = half * 2 + tq
                            ro = tq * 512
                            for k in range(ND):
                                for (jj, slot) in ((2, b + 0), (3, b + 1)):
                                    P.op("pe", lambda t, slot=slot, k=k, tc=tc,
                                         jj=jj: t.matmul(
                                            PS[jj][:, 0:512], wtile(slot, k),
                                            XT[:, k * T + tc * 512:
                                               k * T + (tc + 1) * 512],
                                            start=(k == 0), stop=(k == ND - 1)),
                                         reads=[("wb", slot), *xt_r()],
                                         writes=[("psb", jj)])
                            for (jj, dstbuf, dkey, sckey) in (
                                    (2, QT, "qt", "gsa"), (3, KT, "kt", "gsb")):
                                scalebuf = GSA if sckey == "gsa" else GSB
                                P.op("dve", lambda v, ro=ro, tq=tq, jj=jj,
                                     dstbuf=dstbuf, scalebuf=scalebuf:
                                     v.tensor_mul(
                                        dstbuf[:, ro:ro + 512],
                                        PS[jj][:, 0:512],
                                        scalebuf[:, tq * 512:(tq + 1) * 512]),
                                     reads=[("psb", jj), sckey], writes=[dkey])
                            for c4 in range(4):
                                c = tc * 4 + c4
                                lc = tq * 4 + c4
                                ro128 = ro + c4 * 128
                                vr = (c % 2) * 256
                                kr = (c % 2) * 128
                                pbank = 4 if c % 2 == 0 else 0
                                kvbank = 6 if c % 2 == 0 else 1
                                obank = 5 if c % 2 == 0 else 2
                                cur, nxt = SST[c % 2], SST[(c + 1) % 2]
                                ck, nk = f"sst{c % 2}", f"sst{(c + 1) % 2}"
                                for k in range(ND):
                                    P.op("pe", lambda t, k=k, c=c, b=b: t.matmul(
                                            PS[3][:, 0:256],
                                            XT[:, k * T + c * 128:
                                               k * T + (c + 1) * 128],
                                            WB[:, (b + 2) * 1024 + k * 256:
                                               (b + 2) * 1024 + (k + 1) * 256],
                                            start=(k == 0), stop=(k == ND - 1)),
                                         reads=[("wb", b + 2), ("wb", b + 3),
                                                *xt_r()],
                                         writes=[("psb", 3)])
                                P.op("act", lambda a, vr=vr: a.copy(
                                        VT[:, vr:vr + 256], PS[3][:, 0:256]),
                                     reads=[("psb", 3)], writes=["vt"])
                                P.op("dve", lambda v, ro128=ro128, kr=kr, lc=lc:
                                     v.tensor_scalar(
                                        KH[:, kr:kr + 128],
                                        KT[:, ro128:ro128 + 128],
                                        AEND[:, lc:lc + 1], None, ALU.mult),
                                     reads=["kt", "aend"], writes=["kh"])
                                P.op("pe", lambda t, kr=kr: t.transpose(
                                        PSB[:, 0:128], KH[:, kr:kr + 128],
                                        IDB[:]),
                                     reads=["kh", "ident_b"], writes=[("psb", 7)])
                                P.op("dve", lambda v, kr=kr: v.tensor_copy(
                                        KTOK[:, kr:kr + 128], PSB[:, 0:128]),
                                     reads=[("psb", 7)], writes=["ktok"])
                                P.op("pe", lambda t, ro128=ro128, pbank=pbank:
                                     t.matmul(
                                        PS[pbank][:, 0:128],
                                        KT[:, ro128:ro128 + 128],
                                        QT[:, ro128:ro128 + 128],
                                        start=True, stop=True),
                                     reads=["kt", "qt"], writes=[("psb", pbank)])
                                P.op("dve", lambda v, pbank=pbank: v.tensor_mul(
                                        PTS[:], PS[pbank][:, 0:128],
                                        UAUG[:, 0:128]),
                                     reads=[("psb", pbank), "u_aug"],
                                     writes=["pts"])
                                for blk in range(2):
                                    P.op("pe", lambda t, blk=blk, obank=obank,
                                         ro128=ro128: t.matmul(
                                            PS[obank][:, blk * 128:(blk + 1) * 128],
                                            SSTB[:, blk * 128:(blk + 1) * 128],
                                            QT[:, ro128:ro128 + 128],
                                            start=(blk == 0), stop=False),
                                         reads=["sstb", "qt"],
                                         writes=[("psb", obank)])
                                for blk in range(2):
                                    P.op("pe", lambda t, blk=blk, vr=vr,
                                         obank=obank: t.matmul(
                                            PS[obank][:, blk * 128:(blk + 1) * 128],
                                            VT[:, vr + blk * 128:
                                               vr + (blk + 1) * 128],
                                            PTS[:], start=False, stop=(blk == 1)),
                                         reads=["vt", "pts"],
                                         writes=[("psb", obank)])
                                P.op("pe", lambda t, kr=kr, vr=vr,
                                     kvbank=kvbank: t.matmul(
                                        PS[kvbank][:, 0:256], KTOK[:, kr:kr + 128],
                                        VT[:, vr:vr + 256], start=True, stop=True),
                                     reads=["ktok", "vt"],
                                     writes=[("psb", kvbank)])
                                P.op("dve", lambda v, cur=cur, lc=lc,
                                     kvbank=kvbank: v.scalar_tensor_tensor(
                                        SSTB[:], cur[:], AEND[:, lc:lc + 1],
                                        PS[kvbank][:, 0:256],
                                        ALU.mult, ALU.add),
                                     reads=[ck, "aend", ("psb", kvbank)],
                                     writes=["sstb"])
                                P.op("dve", lambda v, cur=cur, nxt=nxt, lc=lc,
                                     kvbank=kvbank: v.scalar_tensor_tensor(
                                        nxt[:], cur[:], AEND[:, lc:lc + 1],
                                        PS[kvbank][:, 0:256],
                                        ALU.mult, ALU.add),
                                     reads=[ck, "aend", ("psb", kvbank)],
                                     writes=[nk])
                                P.op("act", lambda a, c=c, obank=obank:
                                     a.copy(
                                        OBUF[:, c * 256:(c + 1) * 256],
                                        PS[obank][:, 0:256]),
                                     reads=[("psb", obank)], writes=["obuf"])

                    OBV = OBUF.rearrange("q (c two f) -> q c two f", two=2, f=128)
                    for blk in range(2):
                        for grp in range(4):
                            par = (blk * 4 + grp) % 2
                            osq = OSQ[:, par * 512:(par + 1) * 512]
                            P.op("act", lambda s, blk=blk, grp=grp, osq=osq:
                                 s.activation(
                                    osq, OBV[:, grp * 4:(grp + 1) * 4, blk, :],
                                    AF.Square),
                                 reads=["obuf"], writes=[("osq", par)])
                            P.op("pe", lambda t, grp=grp, blk=blk, osq=osq:
                                 t.matmul(
                                    PS[grp][0:1, 0:512], ONECOLB[:], osq,
                                    start=(blk == 0), stop=(blk == 1)),
                                 reads=[("osq", par), "onecolb"],
                                 writes=[("psb", grp)])
                    for blk in range(2):
                        P.dma("qpool", wslot(b + blk)
                              .rearrange("p (kc f) -> p kc f", f=128),
                              col_slab(g["wg0"], h * 256 + blk * 128, 128),
                              writes=[("wb", b + blk)])
                    for grp in range(4):
                        P.op("act", lambda s, grp=grp: s.activation(
                                OST,
                                PS[grp][0:1, 0:512], AF.Ln,
                                bias=LNEPS[0:1, 0:1], scale=1.0 / 256.0),
                             reads=[("psb", grp), "lneps"],
                             writes=["ost", ("osf", 0)])
                        P.op("act", lambda s: s.activation(
                                INVR[0:1, :], OST, AF.Exp,
                                scale=-0.5),
                             reads=["ost", ("osf", 0)], writes=["invr"])
                        P.op("pe", lambda t, grp=grp: t.matmul(
                                PS[grp][:, 0:512], ONEROWB[0:1, :],
                                INVR[0:1, 0:512], start=True, stop=True),
                             reads=["onerowb", "invr"], writes=[("psb", grp)])
                    for g2 in range(2):
                        for blk in range(2):
                            fb = h * 2 + blk
                            gslot = b + blk
                            for k in range(ND):
                                for gi in range(2):
                                    grp = 2 * g2 + gi
                                    P.op("pe", lambda t, k=k, grp=grp,
                                         gslot=gslot, gi=gi: t.matmul(
                                            PS[4 + gi][:, 0:512],
                                            wtile(gslot, k),
                                            XT[:, k * T + grp * 512:
                                               k * T + (grp + 1) * 512],
                                            start=(k == 0),
                                            stop=(k == ND - 1)),
                                         reads=[("wb", gslot), *xt_r()],
                                         writes=[("psb", 4 + gi)])
                            for gi in range(2):
                                grp = 2 * g2 + gi
                                par = gi
                                gate = GATE[:, par * 512:(par + 1) * 512]
                                osf = OSF[:, par * 512:(par + 1) * 512]
                                P.op("act", lambda s, gate=gate, gi=gi:
                                     s.activation(gate, PS[4 + gi][:, 0:512],
                                                  AF.Silu),
                                     reads=[("psb", 4 + gi)],
                                     writes=[("gate", par)])
                                P.op("dve", lambda v, blk=blk, grp=grp, osf=osf:
                                     v.scalar_tensor_tensor(
                                        osf,
                                        OBV[:, grp * 4:(grp + 1) * 4, blk, :],
                                        RMSWF[:, blk:blk + 1],
                                        PS[grp][:, 0:512],
                                        ALU.mult, ALU.mult),
                                     reads=["obuf", "rmsw_f", ("psb", grp)],
                                     writes=[("osf", par)])
                                P.op("dve", lambda v, fb=fb, grp=grp, osf=osf,
                                     gate=gate: v.tensor_mul(
                                        OPRIME[:, fb * T + grp * 512:
                                               fb * T + (grp + 1) * 512],
                                        osf, gate),
                                     reads=[("osf", par), ("gate", par)],
                                     writes=[("opr", fb)])

                st_scan.close()
                PS.append(st.enter_context(nc.psum_tensor("pa7", [128, 512])))
                for kk in range(ND):
                    for hh2 in range(2):
                        P.dma("qpool",
                              wslot(kk)[:, hh2 * 512:(hh2 + 1) * 512],
                              g["wout"][kk * 128:(kk + 1) * 128,
                                        hh2 * 512:(hh2 + 1) * 512],
                              writes=[("wb", kk)])
                for tg in range(4):
                    for kk in range(ND):
                        for it in range(4):
                            i = tg * 4 + it
                            for dh2 in range(2):
                                bank = it * 2 + dh2
                                P.op("pe", lambda t, kk=kk, i=i, dh2=dh2,
                                     bank=bank: t.matmul(
                                        PS[bank][:, 0:512],
                                        OPRIME[:, kk * T + i * 128:
                                               kk * T + (i + 1) * 128],
                                        wslot(kk)[:, dh2 * 512:(dh2 + 1) * 512],
                                        start=(kk == 0), stop=(kk == ND - 1)),
                                     reads=[("opr", kk), ("wb", kk)],
                                     writes=[("psb", bank)])
                    for it in range(4):
                        i = tg * 4 + it
                        ln_apply(i, PS[it * 2][:, 0:512],
                                 PS[it * 2 + 1][:, 0:512],
                                 f_reads=[("psb", it * 2), ("psb", it * 2 + 1)])

        def stage_l0f():
            with ExitStack() as st:
                def lsb(n, s, d=F32):
                    return st.enter_context(nc.sbuf_tensor(n, s, d))

                def lps(n, s, d=F32):
                    return st.enter_context(nc.psum_tensor(n, s, d))
                PS = [lps(f"pf{i}", [128, 512]) for i in range(8)]
                HBUF = lsb("HBUF", [128, NSLAB * 1024], BF16)
                SIL = lsb("SIL", [128, 1024])
                DACC = lsb("DACC", [128, 8 * 512])
                for i in range(NT):
                    for k in range(ND):
                        tb_ = (i * ND + k) % 2
                        P.op("pe", lambda t, i=i, k=k, tb_=tb_: t.transpose(
                                PS[tb_][:, 0:128],
                                RES[:, i * D + k * 128: i * D + (k + 1) * 128],
                                IDF[:]),
                             reads=[("res", i), "ident_f"],
                             writes=[("psb", tb_)])
                        P.op("dve", lambda v, i=i, k=k, tb_=tb_: v.tensor_copy(
                                XT[:, k * T + i * 128: k * T + (i + 1) * 128],
                                PS[tb_][:, 0:128]),
                             reads=[("psb", tb_)], writes=[("xt", k, i)])
                for half in range(2):
                    t0 = half * 1024
                    for sg in range(12):
                        par = (sg % 2) * 4
                        P.dma("qpool",
                              WB[:, par * 1024:(par + 2) * 1024]
                              .rearrange("p (kc f) -> p kc f", f=256),
                              col_slab(g["wgt"], sg * 256, 256),
                              writes=[("wb", par), ("wb", par + 1)])
                        P.dma("qpool",
                              WB[:, (par + 2) * 1024:(par + 4) * 1024]
                              .rearrange("p (kc f) -> p kc f", f=256),
                              col_slab(g["wut"], sg * 256, 256),
                              writes=[("wb", par + 2), ("wb", par + 3)])
                        for s4 in range(2):
                            s_ = sg * 2 + s4
                            for k in range(ND):
                                for tc2 in range(2):
                                    t1 = t0 + tc2 * 512
                                    P.op("pe", lambda t, k=k, t1=t1, par=par,
                                         s4=s4, tc2=tc2: t.matmul(
                                            PS[4 + tc2][:, 0:512],
                                            WB[:, par * 1024 + (k * 2 + s4) * 128:
                                               par * 1024 + (k * 2 + s4 + 1) * 128],
                                            XT[:, k * T + t1: k * T + t1 + 512],
                                            start=(k == 0), stop=(k == ND - 1)),
                                         reads=[("wb", par), ("wb", par + 1),
                                                *[("xt", k, t1 // 128 + ii)
                                                  for ii in range(4)]],
                                         writes=[("psb", 4 + tc2)])
                            for k in range(ND):
                                for tc2 in range(2):
                                    t1 = t0 + tc2 * 512
                                    P.op("pe", lambda t, k=k, t1=t1, par=par,
                                         s4=s4, tc2=tc2: t.matmul(
                                            PS[6 + tc2][:, 0:512],
                                            WB[:, (par + 2) * 1024 +
                                               (k * 2 + s4) * 128:
                                               (par + 2) * 1024 +
                                               (k * 2 + s4 + 1) * 128],
                                            XT[:, k * T + t1: k * T + t1 + 512],
                                            start=(k == 0), stop=(k == ND - 1)),
                                         reads=[("wb", par + 2), ("wb", par + 3),
                                                *[("xt", k, t1 // 128 + ii)
                                                  for ii in range(4)]],
                                         writes=[("psb", 6 + tc2)])
                            for tc2 in range(2):
                                sil = SIL[:, tc2 * 512:(tc2 + 1) * 512]
                                P.op("act", lambda s, sil=sil, tc2=tc2:
                                     s.activation(sil, PS[4 + tc2][:, 0:512],
                                                  AF.Silu, scale=1.0 / 127.0),
                                     reads=[("psb", 4 + tc2)],
                                     writes=[("sil", tc2)])
                                P.op("dve", lambda v, s_=s_, tc2=tc2, sil=sil:
                                     v.scalar_tensor_tensor(
                                        HBUF[:, s_ * 1024 + tc2 * 512:
                                             s_ * 1024 + (tc2 + 1) * 512],
                                        PS[6 + tc2][:, 0:512], 1.0 / 127.0, sil,
                                        ALU.mult, ALU.mult),
                                     reads=[("psb", 6 + tc2), ("sil", tc2)],
                                     writes=[("hb", s_, tc2)])
                    for dh2 in range(2):
                        for sg in range(6):
                            par = (sg % 2) * 2
                            for hc in range(2):
                                P.dma("qpool",
                                      WB[:, (par + hc) * 1024:
                                         (par + hc + 1) * 1024]
                                      .rearrange("p (kc f) -> p kc f", f=512),
                                      g["wdt"][sg * 512 + hc * 256:
                                               sg * 512 + (hc + 1) * 256,
                                               dh2 * 512:(dh2 + 1) * 512]
                                      .rearrange("(kc p) f -> p kc f", p=128),
                                      writes=[("wb", par + hc)])
                            for s4 in range(4):
                                s_ = sg * 4 + s4
                                for it in range(8):
                                    P.op("pe", lambda t, s_=s_, it=it,
                                         par=par, s4=s4: t.matmul(
                                            PS[it][:, 0:512],
                                            HBUF[:, s_ * 1024 + it * 128:
                                                 s_ * 1024 + (it + 1) * 128],
                                            WB[:, par * 1024 + s4 * 512:
                                               par * 1024 + (s4 + 1) * 512],
                                            start=(s_ == 0),
                                            stop=(s_ == NSLAB - 1)),
                                         reads=[("hb", s_, it // 4),
                                                ("wb", par), ("wb", par + 1)],
                                         writes=[("psb", it)])
                        if dh2 == 0:
                            for it in range(8):
                                P.op("dve", lambda v, it=it: v.tensor_copy(
                                        DACC[:, it * 512:(it + 1) * 512],
                                        PS[it][:, 0:512]),
                                     reads=[("psb", it)],
                                     writes=[("dacc", it)])
                    for it in range(8):
                        i = half * 8 + it
                        ln_apply(i, DACC[:, it * 512:(it + 1) * 512],
                                 PS[it][:, 0:512],
                                 f_reads=[("dacc", it), ("psb", it)])
        def stage_l1a():
            with ExitStack() as st:
                def lsb(n, s, d=F32):
                    return st.enter_context(nc.sbuf_tensor(n, s, d))

                def lps(n, s, d=F32):
                    return st.enter_context(nc.psum_tensor(n, s, d))
                SPa = lps("SPa", [128, 1024])
                SPb = lps("SPb", [128, 1024])
                PO = lps("PO", [128, 1024])
                PBC = lps("PBC", [128, 512])
                SPS = lps("SPS", [128, 512])
                _BK = [(SPa, 0), (SPa, 512), (SPb, 0), (SPb, 512),
                       (PO, 0), (PO, 512), (PBC, 0), (SPS, 0)]

                def bk(j, lo=0, hi=512, p0=0, p1=128):
                    tns, off = _BK[j]
                    return tns[p0:p1, off + lo: off + hi]
                CQD = WB[:, 2 * 1024:3 * 1024]
                SQD = WB[:, 3 * 1024:4 * 1024]
                QTMP = WB[:, 4 * 1024:5 * 1024]
                RECA = WB[0:16, 5 * 1024:5 * 1024 + 512]
                QF = lsb("QF", [128, 1024], BF16)
                ATT = lsb("ATT", [128, 8 * 512], BF16)
                PTB = lsb("PTB", [128, 2048], BF16)
                KF = lsb("KF", [128, 8 * T], BF16)
                VG = lsb("VG", [128, NT * 16 * 65], BF16)
                vones = VG.rearrange("q (i h f) -> q i h f", h=16, f=65)
                TB = lsb("TBr", [128, 512], BF16)
                DSTG = lsb("DSTG", [1, 512], BF16)
                DENA = lsb("DENA", [16, 512], BF16)
                HSEL = lsb("HSEL", [16, 1024], BF16)
                PERMB = lsb("PERMB", [128, 128], BF16)

                P.dma("qpool", HSEL[:], g["hselb"], writes=["hsel"])
                P.dma("qpool", PERMB[:], g["permr"], writes=["permb"])

                def load_cs(par, c0):
                    P.dma("qpool", CQD[:, par * 512:(par + 1) * 512],
                          g["cos_f"][:, c0:c0 + 512],
                          writes=[("cq", par), ("wb", 2)])
                    P.dma("qpool", SQD[:, par * 512:(par + 1) * 512],
                          g["sin_f"][:, c0:c0 + 512],
                          writes=[("sq", par), ("wb", 3)])

                def load_pair(slot, base, p):
                    dst = WB[:, slot * 1024:(slot + 1) * 1024]
                    dv = dst.rearrange("q (kc two f) -> q kc two f", two=2, f=64)
                    P.dma("qpool", dv[:, :, 0, :],
                          col_slab(g["wqkv"], base + p * 64, 64),
                          writes=[("wb", slot)])
                    P.dma("qpool", dv[:, :, 1, :],
                          col_slab(g["wqkv"], base + (p + 8) * 64, 64),
                          writes=[("wb", slot)])

                def rope_combine(dst_ap, dkey, par, pb, cpar):
                    co = CQD[:, cpar * 512:(cpar + 1) * 512]
                    si = SQD[:, cpar * 512:(cpar + 1) * 512]
                    qt = QTMP[:, par * 512:(par + 1) * 512]
                    P.op("dve", lambda v: v.tensor_mul(qt, qt, co),
                         reads=[("qtmp", par), ("cq", cpar)],
                         writes=[("qtmp", par)])
                    P.op("dve", lambda v: v.tensor_mul(TB[:], bk(pb), si),
                         reads=[("psb", pb), ("sq", cpar)],
                         writes=["tbr"])
                    P.op("dve", lambda v: v.tensor_add(dst_ap, qt, TB[:]),
                         reads=[("qtmp", par), "tbr"], writes=[dkey])

                def evac_rot(par, src_bank, rot_bank):
                    qt = QTMP[:, par * 512:(par + 1) * 512]
                    P.op("dve", lambda v: v.tensor_copy(qt, bk(src_bank)),
                         reads=[("psb", src_bank)],
                         writes=[("qtmp", par), ("wb", 4)])
                    P.op("pe", lambda t: t.matmul(
                            bk(rot_bank), PERMB[:], qt, start=True, stop=True),
                         reads=["permb", ("qtmp", par)],
                         writes=[("psb", rot_bank)])

                for i in range(NT):
                    for k in range(ND):
                        tb_ = 6 + ((i * ND + k) % 2)
                        P.op("pe", lambda t, i=i, k=k, tb_=tb_: t.transpose(
                                bk(tb_, 0, 128),
                                RES[:, i * D + k * 128: i * D + (k + 1) * 128],
                                IDF[:]),
                             reads=[("res", i), "ident_f"],
                             writes=[("psb", tb_)])
                        P.op("dve", lambda v, i=i, k=k, tb_=tb_: v.tensor_copy(
                                XT[:, k * T + i * 128: k * T + (i + 1) * 128],
                                bk(tb_, 0, 128)),
                             reads=[("psb", tb_)], writes=[("xt", k, i)])

                load_pair(0, D, 0)
                for p in range(8):
                    sl = p % 2
                    for th in range(2):
                        tcA, tcB = 2 * th, 2 * th + 1
                        load_cs(0, tcA * 512)
                        load_cs(1, tcB * 512)
                        if th == 0 and p + 1 < 8:
                            load_pair((p + 1) % 2, D, p + 1)
                        for k in range(ND):
                            for (tc, jn) in ((tcA, 0), (tcB, 2)):
                                P.op("pe", lambda t, k=k, tc=tc, sl=sl, jn=jn:
                                     t.matmul(
                                        bk(jn), wtile(sl, k),
                                        XT[:, k * T + tc * 512:
                                           k * T + (tc + 1) * 512],
                                        start=(k == 0), stop=(k == ND - 1)),
                                     reads=[("wb", sl),
                                            *[("xt", k, tc * 4 + ii)
                                              for ii in range(4)]],
                                     writes=[("psb", jn)])
                        for (tc, par, jn) in ((tcA, 0, 0), (tcB, 1, 2)):
                            evac_rot(par, jn, jn + 1)
                            rope_combine(
                                KF[:, p * T + tc * 512:
                                   p * T + (tc + 1) * 512],
                                ("kf", p), par, jn + 1, par)

                P.op("dve", lambda v: v.memset(vones[:, :, :, 64:65], 1.0),
                     writes=["vg"])
                for vh in range(2):
                    for k in range(ND):
                        sl = vh * 4 + k % 4
                        extra = ()
                        if sl == 2:
                            extra = (("cq", 0), ("cq", 1))
                        elif sl == 3:
                            extra = (("sq", 0), ("sq", 1))
                        elif sl == 4:
                            extra = (("qtmp", k // 4),)
                        elif sl == 5 and k // 4 == 0:
                            extra = ("reca",)
                        P.dma("qpool",
                              WB[:, sl * 1024 + (k // 4) * 512:
                                 sl * 1024 + (k // 4) * 512 + 512],
                              g["wqkv"][k * 128:(k + 1) * 128,
                                        2 * D + vh * 512: 2 * D + (vh + 1) * 512],
                              writes=[("wb", sl), *extra])
                for i in range(NT):
                    for k in range(ND):
                        for vh in range(2):
                            sl = vh * 4 + k % 4
                            vb = 2 + vh + 2 * (i % 2)
                            P.op("pe", lambda t, k=k, i=i, sl=sl, vb=vb:
                                 t.matmul(
                                    bk(vb),
                                    XT[:, k * T + i * 128: k * T + (i + 1) * 128],
                                    WB[:, sl * 1024 + (k // 4) * 512:
                                       sl * 1024 + (k // 4) * 512 + 512],
                                    start=(k == 0), stop=(k == ND - 1)),
                                 reads=[("wb", sl), ("xt", k, i)],
                                 writes=[("psb", vb)])
                    for vh in range(2):
                        vb = 2 + vh + 2 * (i % 2)
                        hv = vones[:, i, vh * 8:(vh + 1) * 8, 0:64]
                        P.op("dve", lambda v, hv=hv, vb=vb: v.tensor_copy(
                                hv, bk(vb).rearrange(
                                    "q (h f) -> q h f", f=64)),
                             reads=[("psb", vb)], writes=["vg"])

                load_cs(0, 0)

                def q_proj(qc, p, ks):
                    sl = p % 2
                    for k in ks:
                        P.op("pe", lambda t, k=k, qc=qc, sl=sl: t.matmul(
                                bk(6), wtile(sl, k),
                                XT[:, k * T + qc * 512:
                                   k * T + (qc + 1) * 512],
                                start=(k == 0), stop=(k == ND - 1)),
                             reads=[("wb", sl),
                                    *[("xt", k, qc * 4 + ii)
                                      for ii in range(4)]],
                             writes=[("psb", 6)])

                def q_finish(qc, p):
                    par = p % 2
                    evac_rot(par, 6, 7)
                    rope_combine(QF[:, par * 512:(par + 1) * 512],
                                 ("qf", par), par, 7, qc % 2)

                for qc in range(4):
                    if qc == 0:
                        load_pair(0, 0, 0)
                        q_proj(0, 0, range(ND))
                        q_finish(0, 0)
                    for p in range(8):
                        qro = (p % 2) * 512
                        def emit_s(kv):
                            pp = kv % 2
                            jA = 2 * pp
                            stens = SPa if pp == 0 else SPb
                            po = pp * 1024
                            P.op("pe", lambda t, p=p, kv=kv, stens=stens,
                                 qro=qro: t.matmul(
                                    stens[:, 0:512],
                                    KF[0:64, p * T + kv * 128:
                                       p * T + (kv + 1) * 128],
                                    QF[0:64, qro:qro + 512],
                                    start=True, stop=True),
                                 reads=[("kf", p), ("qf", p % 2)],
                                 writes=[("psb", jA)])
                            P.op("pe", lambda t, p=p, kv=kv, stens=stens,
                                 qro=qro: t.matmul(
                                    stens[:, 512:1024],
                                    KF[64:128, p * T + kv * 128:
                                       p * T + (kv + 1) * 128],
                                    QF[64:128, qro:qro + 512],
                                    start=True, stop=True),
                                 reads=[("kf", p), ("qf", p % 2)],
                                 writes=[("psb", jA + 1)])
                            P.op("act", lambda s, stens=stens, po=po:
                                 s.activation(
                                    PTB[:, po:po + 512], stens[:, 0:512],
                                    AF.Exp, scale=0.125),
                                 reads=[("psb", jA)],
                                 writes=[("ptb", pp, 0)])
                            P.op("act", lambda s, stens=stens, po=po:
                                 s.activation(
                                    PTB[:, po + 512:po + 1024],
                                    stens[:, 512:1024],
                                    AF.Exp, scale=0.125),
                                 reads=[("psb", jA + 1)],
                                 writes=[("ptb", pp, 1)])

                        def emit_pv(kv):
                            pp = kv % 2
                            po = pp * 1024
                            P.op("pe", lambda t, kv=kv, p=p, po=po: t.matmul(
                                    PO[0:65, 0:512], vones[:, kv, p, :],
                                    PTB[:, po:po + 512],
                                    start=(kv == 0), stop=(kv == NT - 1)),
                                 reads=["vg", ("ptb", pp, 0)],
                                 writes=[("psb", 4)])
                            P.op("pe", lambda t, kv=kv, p=p, po=po: t.matmul(
                                    PO[0:65, 512:1024], vones[:, kv, p + 8, :],
                                    PTB[:, po + 512:po + 1024],
                                    start=(kv == 0), stop=(kv == NT - 1)),
                                 reads=["vg", ("ptb", pp, 1)],
                                 writes=[("psb", 5)])

                        emit_s(0)
                        for kv in range(NT):
                            if kv + 1 < NT:
                                emit_s(kv + 1)
                            if kv == 0 and p + 1 < 8:
                                load_pair((p + 1) % 2, 0, p + 1)
                            if kv == 0 and p == 7 and qc < 3:
                                load_pair(0, 0, 0)
                                load_cs((qc + 1) % 2, (qc + 1) * 512)
                            if 5 <= kv < 13 and p + 1 < 8:
                                q_proj(qc, p + 1, [kv - 5])
                            if kv == 13 and p + 1 < 8:
                                q_finish(qc, p + 1)
                            emit_pv(kv)
                        for hh in range(2):
                            h = p + 8 * hh
                            oj = 4 + hh
                            hr = (h % 2) * 64
                            hb = (h // 2) * 512
                            P.op("dve", lambda v, hr=hr, hb=hb, oj=oj:
                                 v.tensor_copy(
                                    ATT[hr:hr + 64, hb:hb + 512],
                                    bk(oj, p0=0, p1=64)),
                                 reads=[("psb", oj)], writes=[("att", h)])
                        P.op("dve", lambda v: v.tensor_copy(
                                DSTG[0:1, :], PO[64:65, 0:512]),
                             reads=[("psb", 4)], writes=["dstg"])
                        P.dma("qsp", DENA[2 * p:2 * p + 1, :], DSTG[0:1, :],
                              reads=["dstg"], writes=[("den", 2 * p)])
                        P.op("dve", lambda v: v.tensor_copy(
                                DSTG[0:1, :], PO[64:65, 512:1024]),
                             reads=[("psb", 5)], writes=["dstg"])
                        P.dma("qsp", DENA[2 * p + 1:2 * p + 2, :], DSTG[0:1, :],
                              reads=["dstg"], writes=[("den", 2 * p + 1)])

                    def _recip(v):
                        with nc.allow_low_precision(
                                reason="bf16 softmax denom broadcast"):
                            return v.reciprocal(RECA, DENA[:])
                    P.op("dve", _recip,
                         reads=[("den", r) for r in range(16)],
                         writes=["reca", ("wb", 5)])
                    for p in range(8):
                        nb = 6 + (p % 2)
                        P.op("pe", lambda t, p=p, nb=nb: t.matmul(
                                bk(nb), HSEL[0:16, p * 128:(p + 1) * 128],
                                RECA, start=True, stop=True),
                             reads=["hsel", "reca"], writes=[("psb", nb)])
                        for hh in range(2):
                            h = p + 8 * hh
                            hr = (p % 2) * 64
                            hb = (p // 2 + 4 * hh) * 512
                            rb = hh * 64
                            P.op("dve", lambda v, hr=hr, hb=hb, nb=nb, rb=rb:
                                 v.tensor_mul(
                                    ATT[hr:hr + 64, hb:hb + 512],
                                    ATT[hr:hr + 64, hb:hb + 512],
                                    bk(nb, p0=rb, p1=rb + 64)),
                                 reads=[("psb", nb), ("att", h)],
                                 writes=[("att", h)])
                    for kk in range(ND):
                        if kk < 4:
                            ph = (2 * kk, 2 * kk + 1)
                        else:
                            ph = (2 * (kk - 4) + 8, 2 * (kk - 4) + 9)
                        slot = 6 + (kk % 2)
                        for hh2 in range(2):
                            P.dma("qpool",
                                  wslot(slot)[:, hh2 * 512:(hh2 + 1) * 512],
                                  g["wo"][kk * 128:(kk + 1) * 128,
                                          hh2 * 512:(hh2 + 1) * 512],
                                  writes=[("wb", slot)])
                        for it in range(4):
                            for dh2 in range(2):
                                bank = it * 2 + dh2
                                P.op("pe", lambda t, kk=kk, it=it, dh2=dh2,
                                     bank=bank, slot=slot: t.matmul(
                                        bk(bank),
                                        ATT[:, kk * 512 + it * 128:
                                            kk * 512 + (it + 1) * 128],
                                        wslot(slot)[:, dh2 * 512:(dh2 + 1) * 512],
                                        start=(kk == 0), stop=(kk == ND - 1)),
                                     reads=[("att", ph[0]), ("att", ph[1]),
                                            ("wb", slot)],
                                     writes=[("psb", bank)])
                    for it in range(4):
                        i = qc * 4 + it
                        ln_apply(i, bk(it * 2), bk(it * 2 + 1),
                                 f_reads=[("psb", it * 2), ("psb", it * 2 + 1)])
                    if qc < 3:
                        q_proj(qc + 1, 0, range(ND))
                        q_finish(qc + 1, 0)

        def stage_l1f():
            with ExitStack() as st:
                def lsb(n, s, d=F32):
                    return st.enter_context(nc.sbuf_tensor(n, s, d))

                def lps(n, s, d=F32):
                    return st.enter_context(nc.psum_tensor(n, s, d))
                PS = [lps(f"pm{i}", [128, 512]) for i in range(8)]
                PROBS = lsb("PROBS", [128, NT * 8])
                MASK = lsb("MASK", [128, NT * 8])
                SSE = lsb("SSE", [128, NT * 8])
                SSET = lsb("SSET", [8, NT * 128], BF16)
                MX8 = lsb("MX8", [128, 8])
                KNK = lsb("KNK", [128, 8])
                KEEP = lsb("KEEP", [128, NT])
                CMPB = lsb("CMPB", [128, 8])
                CNT = lsb("CNT", [1, 8])
                CC8 = lsb("CC8", [8, 8])
                CALL = lsb("CALL", [8, 8])
                OH = lsb("OH", [1, 8])
                MLT = lsb("MLT", [8, 128])
                BM = lsb("BM", [8, 1024], BF16)
                WGM = lsb("WGM", [128, 64], BF16)
                ACC = lsb("ACC", [128, 8 * 1024])

                P.dma("qsp", OH[:], g["onehot"], writes=["oh"])
                P.dma("qsp", MLT[:], g["masklt_rep"], writes=["mlt"])
                P.dma("qpool", BM[:], g["bm"], writes=["bm"])
                P.dma("qpool", WGM[:].rearrange("p (kc f) -> p kc f", f=8),
                      col_slab(g["wgm"], 0, 8), writes=["wgm"])

                for i in range(NT):
                    for k in range(ND):
                        tb_ = 6 + ((i * ND + k) % 2)
                        P.op("pe", lambda t, i=i, k=k, tb_=tb_: t.transpose(
                                PS[tb_][:, 0:128],
                                RES[:, i * D + k * 128: i * D + (k + 1) * 128],
                                IDF[:]),
                             reads=[("res", i), "ident_f"],
                             writes=[("psb", tb_)])
                        P.op("dve", lambda v, i=i, k=k, tb_=tb_: v.tensor_copy(
                                XT[:, k * T + i * 128: k * T + (i + 1) * 128],
                                PS[tb_][:, 0:128]),
                             reads=[("psb", tb_)], writes=[("xt", k, i)])
                    rb_ = 3 * (i % 2)
                    for k in range(ND):
                        P.op("pe", lambda t, k=k, i=i, rb_=rb_: t.matmul(
                                PS[rb_][:, 0:8],
                                XT[:, k * T + i * 128: k * T + (i + 1) * 128],
                                WGM[:, k * 8:(k + 1) * 8],
                                start=(k == 0), stop=(k == ND - 1)),
                             reads=["wgm", *xt_r(i)], writes=[("psb", rb_)])
                    P.op("act", lambda s, i=i, rb_=rb_: s.activation(
                            PROBS[:, i * 8:(i + 1) * 8], PS[rb_][:, 0:8],
                            AF.Exp, accum_out=LNC[:, 0:1]),
                         reads=[("psb", rb_)], writes=["probs", "lnc0"])
                    P.op("dve", lambda v: v.reciprocal(LNC[:, 1:2], LNC[:, 0:1]),
                         reads=["lnc0"], writes=["lnc1"])
                    P.op("dve", lambda v, i=i: v.tensor_scalar(
                            PROBS[:, i * 8:(i + 1) * 8],
                            PROBS[:, i * 8:(i + 1) * 8], LNC[:, 1:2], None,
                            ALU.mult),
                         reads=["probs", "lnc1"], writes=["probs"])
                    P.op("dve", lambda v, i=i: v.max(
                            MX8[:], PROBS[:, i * 8:(i + 1) * 8]),
                         reads=["probs"], writes=["mx8"])
                    P.op("dve", lambda v: v.memset(MX8[:, 1:8], -2.0),
                         writes=["mx8"])
                    P.op("dve", lambda v, i=i: v.match_replace(
                            KNK[:], in_to_replace=MX8[:],
                            in_values=PROBS[:, i * 8:(i + 1) * 8],
                            imm_value=-5.0),
                         reads=["probs", "mx8"], writes=["knk"])
                    P.op("dve", lambda v, i=i: v.tensor_tensor(
                            MASK[:, i * 8:(i + 1) * 8],
                            PROBS[:, i * 8:(i + 1) * 8], KNK[:], ALU.not_equal),
                         reads=["probs", "knk"], writes=[("mask", i)])
                    P.op("dve", lambda v, i=i: v.tensor_mul(
                            SSE[:, i * 8:(i + 1) * 8],
                            MASK[:, i * 8:(i + 1) * 8],
                            PROBS[:, i * 8:(i + 1) * 8]),
                         reads=[("mask", i), "probs"], writes=["sse"])
                    P.op("pe", lambda t, i=i: t.matmul(
                            PS[1][0:1, 0:8], ONE1[:, 0:1],
                            MASK[:, i * 8:(i + 1) * 8],
                            start=(i == 0), stop=(i == NT - 1)),
                         reads=[("mask", i), "one1"], writes=[("psb", 1)])
                    P.op("pe", lambda t, i=i: t.transpose(
                            PS[2][0:8, 0:128], SSE[:, i * 8:(i + 1) * 8], IDF[:]),
                         reads=["sse", "ident_f"], writes=[("psb", 2)])
                    P.op("dve", lambda v, i=i: v.tensor_copy(
                            SSET[0:8, i * 128:(i + 1) * 128], PS[2][0:8, 0:128]),
                         reads=[("psb", 2)], writes=["sset"])

                P.op("dve", lambda v: v.tensor_copy(CNT[:], PS[1][0:1, 0:8]),
                     reads=[("psb", 1)], writes=["cnt"])
                P.op("pe", lambda t: t.matmul(PS[3][0:8, 0:8], OH[:], CNT[:],
                                              start=True, stop=True),
                     reads=["oh", "cnt"], writes=[("psb", 3)])
                P.op("dve", lambda v: v.tensor_copy(CC8[:], PS[3][0:8, 0:8]),
                     reads=[("psb", 3)], writes=["cc8"])
                P.dma("qpool", cc_in_t.ap(), CC8[:], reads=["cc8"],
                      writes=["ccin"])
                P.cc(lambda gp: gp.collective_compute(
                        "AllReduce", ALU.add,
                        replica_groups=[[0, 1, 2, 3, 4, 5, 6, 7]],
                        ins=[cc_in_t.ap().opt()], outs=[cc_out_t.ap().opt()]),
                     reads=["ccin"], writes=["ccout"])
                P.dma("qpool", CALL[:], cc_out_t.ap(), reads=["ccout"],
                      writes=["call"])

                def keep_tile(i):
                    P.op("pe", lambda t, i=i: t.matmul(
                            PS[0][:, 0:8], LSTRF[:],
                            MASK[:, i * 8:(i + 1) * 8], start=True, stop=False),
                         reads=["l_strict", ("mask", i)], writes=[("psb", 0)])
                    for j in range(i):
                        P.op("pe", lambda t, j=j: t.matmul(
                                PS[0][:, 0:8], ONESF[:],
                                MASK[:, j * 8:(j + 1) * 8],
                                start=False, stop=False),
                             reads=["ones_f", ("mask", j)], writes=[("psb", 0)])
                    P.op("pe", lambda t: t.matmul(
                            PS[0][:, 0:8], MLT[:], CALL[0:8, 0:8],
                            start=False, stop=True),
                         reads=["mlt", "call"], writes=[("psb", 0)])
                    P.op("dve", lambda v: v.tensor_single_scalar(
                            CMPB[:], PS[0][:, 0:8], float(CAP) - 0.5, ALU.is_le),
                         reads=[("psb", 0)], writes=["cmpb"])
                    P.op("dve", lambda v, i=i: v.tensor_mul(
                            CMPB[:], CMPB[:], MASK[:, i * 8:(i + 1) * 8]),
                         reads=["cmpb", ("mask", i)], writes=["cmpb"])
                    P.op("dve", lambda v, i=i: v.reduce_sum(
                            KEEP[:, i:i + 1], CMPB[:], AX.X),
                         reads=["cmpb"], writes=[("keep", i)])

                for half in range(2):
                    i0 = half * 8
                    for it in range(8):
                        i = i0 + it
                        b0, b1 = 4 + (it % 2) * 2, 5 + (it % 2) * 2
                        P.op("pe", lambda t, i=i, b0=b0: t.matmul(
                                PS[b0][:, 0:512],
                                SSET[0:8, i * 128:(i + 1) * 128],
                                BM[0:8, 0:512], start=True, stop=True),
                             reads=["sset", "bm"], writes=[("psb", b0)])
                        P.op("pe", lambda t, i=i, b1=b1: t.matmul(
                                PS[b1][:, 0:512],
                                SSET[0:8, i * 128:(i + 1) * 128],
                                BM[0:8, 512:1024], start=True, stop=True),
                             reads=["sset", "bm"], writes=[("psb", b1)])
                        P.op("dve", lambda v, it=it, b0=b0: v.tensor_copy(
                                ACC[:, it * 1024: it * 1024 + 512],
                                PS[b0][:, 0:512]),
                             reads=[("psb", b0)], writes=[("acc", it)])
                        P.op("dve", lambda v, it=it, b1=b1: v.tensor_copy(
                                ACC[:, it * 1024 + 512:(it + 1) * 1024],
                                PS[b1][:, 0:512]),
                             reads=[("psb", b1)], writes=[("acc", it)])
                    for e in range(EXP):
                        for g2 in range(2):
                            base = g2 * 4
                            for kc in range(4):
                                r0 = g2 * 512 + kc * 128
                                P.dma("qpool",
                                      WB[:, (base + kc) * 1024:
                                         (base + kc + 1) * 1024],
                                      g["wet"][e, r0:r0 + 128, :],
                                      writes=[("wb", base + kc)])
                        for tq4 in range(2):
                            for g2 in range(2):
                                base = g2 * 4
                                for it4 in range(4):
                                    it = tq4 * 4 + it4
                                    i = i0 + it
                                    for dh2 in range(2):
                                        bank = it4 * 2 + dh2
                                        for k4 in range(4):
                                            kk = g2 * 4 + k4
                                            P.op("pe", lambda t, kk=kk, i=i,
                                                 dh2=dh2, bank=bank, base=base,
                                                 k4=k4: t.matmul(
                                                    PS[bank][:, 0:512],
                                                    XT[:, kk * T + i * 128:
                                                       kk * T + (i + 1) * 128],
                                                    WB[:, (base + k4) * 1024 +
                                                       dh2 * 512:
                                                       (base + k4) * 1024 +
                                                       (dh2 + 1) * 512],
                                                    start=(kk == 0),
                                                    stop=(kk == ND - 1)),
                                                 reads=[("wb", base + k4),
                                                        *xt_r()],
                                                 writes=[("psb", bank)])
                                        if g2 == 1:
                                            P.op("dve", lambda v, it=it,
                                                 dh2=dh2, i=i, e=e, bank=bank:
                                                 v.scalar_tensor_tensor(
                                                    ACC[:, it * 1024 + dh2 * 512:
                                                        it * 1024 +
                                                        (dh2 + 1) * 512],
                                                    PS[bank][:, 0:512],
                                                    SSE[:, i * 8 + e:
                                                        i * 8 + e + 1],
                                                    ACC[:, it * 1024 + dh2 * 512:
                                                        it * 1024 +
                                                        (dh2 + 1) * 512],
                                                    ALU.mult, ALU.add),
                                                 reads=[("psb", bank), "sse",
                                                        ("acc", it)],
                                                 writes=[("acc", it)])
                    for it in range(8):
                        i = i0 + it
                        keep_tile(i)
                        ln_apply(i, ACC[:, it * 1024: it * 1024 + 512],
                                 ACC[:, it * 1024 + 512:(it + 1) * 1024],
                                 f_reads=[("acc", it)],
                                 keep_col=KEEP[:, i:i + 1],
                                 keep_reads=[("keep", i)])

        build_xt("s0")
        P.barrier()
        if "l0a" in stages:
            stage_l0a()
            tap("y1", res_r)
            P.barrier()
        if "l0f" in stages:
            stage_l0f()
            tap("y2", res_r)
            P.barrier()
        if "l1a" in stages:
            stage_l1a()
            tap("y3", res_r)
            P.barrier()
        if "l1f" in stages:
            stage_l1f()
            tap("y4", res_r)
            P.barrier()

        for i in range(NT):
            P.dma("qsp", out[i * 128:(i + 1) * 128, :], RES[:, i * D:(i + 1) * D],
                  reads=[("res", i)])
        P.barrier(engines=("sp",))

        block = stack.enter_context(nc.Block())
        P.finish(block)

    return nc


def kernel(**inputs):
    in_maps = make_in_maps(inputs)
    nc = build_program()
    res = run_bass_kernel_spmd(nc, in_maps, core_ids=list(range(8)))
    return np.stack([np.asarray(r["out"], dtype=np.float32)
                     for r in res.results], 0)


# version 21 (final)
# speedup vs baseline: 1.0189x; 1.0189x over previous
import numpy as np
import ml_dtypes
from contextlib import ExitStack

import concourse.bass as bass
import concourse.mybir as mybir
from concourse.bass_utils import run_bass_kernel_spmd

F32 = mybir.dt.float32
BF16 = mybir.dt.bfloat16
I32 = mybir.dt.int32
AF = mybir.ActivationFunctionType
ALU = mybir.AluOpType
AX = mybir.AxisListType

T = 2048
D = 1024
NT = T // 128
ND = D // 128
CH = 128
NCH = T // CH
HID = 3072
NSLAB = HID // 128
EXP = 8
CAP = 2048
EPS = 1e-5

ENGS = ("pe", "act", "dve", "pool", "sp")
SAME_ENGINE_SYNC = True
QUEUES = ("qsp", "qact", "qpool", "qcc")
Q_ISSUER = {"qsp": "sp", "qact": "act", "qpool": "pool", "qcc": "pool"}
UNIT = dict({e: 1 for e in ENGS}, qsp=16, qact=16, qpool=16, qcc=1)


class Cmd:
    __slots__ = ("key", "idx", "build", "waits")

    def __init__(self, key, idx, build):
        self.key, self.idx, self.build, self.waits = key, idx, build, []


class Prog:

    NLANE = 6

    def __init__(self, nc, stack, same_engine_sync=True):
        self.nc = nc
        self.same_engine_sync = same_engine_sync
        self.tick = {k: stack.enter_context(nc.semaphore(f"tk_{k}"))
                     for k in ENGS}
        self.qtick = {}
        for q in QUEUES:
            nl = 1 if q == "qcc" else self.NLANE
            self.qtick[q] = [stack.enter_context(nc.semaphore(f"tk_{q}_{l}"))
                             for l in range(nl)]
        self.cmds = {e: [] for e in ENGS}
        self.n = {k: 0 for k in (*ENGS, *QUEUES)}
        self.marks = {k: set() for k in ENGS}
        self.last_wait = {e: {} for e in ENGS}
        self.res = {}

    def _qwait(self, cmd, waiter, queue, lane, val):
        k = ("q", queue, lane)
        if self.last_wait[waiter].get(k, -1) >= val:
            return
        self.last_wait[waiter][k] = val
        cmd.waits.append(("q", queue, lane, val))

    def _ewait(self, cmd, waiter, eng, idx):
        if eng == waiter:
            if waiter == "pe" or not self.same_engine_sync:
                return
        cand, limit = None, self.n[eng]
        for m in self.marks[eng]:
            if idx <= m < limit and (cand is None or m < cand):
                cand = m
        if cand is None:
            self.marks[eng].add(idx)
            cand = idx
        if self.last_wait[waiter].get(eng, -1) >= cand:
            return
        self.last_wait[waiter][eng] = cand
        cmd.waits.append(("e", eng, cand))

    def _wait(self, cmd, waiter, prod):
        pkey, pidx = prod
        if pkey in ENGS:
            self._ewait(cmd, waiter, pkey, pidx)
        else:
            nl = 1 if pkey == "qcc" else self.NLANE
            lane, ordinal = pidx % nl, pidx // nl
            unit = 1 if pkey == "qcc" else 16
            self._qwait(cmd, waiter, pkey, lane, (ordinal + 1) * unit)

    def _deps(self, cmd, waiter, reads, writes):
        for r in reads:
            e = self.res.get(r)
            if e and e["w"] is not None:
                self._wait(cmd, waiter, e["w"])
        for w in writes:
            e = self.res.get(w)
            if e:
                if e["w"] is not None:
                    self._wait(cmd, waiter, e["w"])
                for rr in e["r"]:
                    self._wait(cmd, waiter, rr)

    def _record(self, key, idx, reads, writes):
        for r in reads:
            self.res.setdefault(r, {"w": None, "r": []})["r"].append((key, idx))
        for w in writes:
            e = self.res.setdefault(w, {"w": None, "r": []})
            e["w"] = (key, idx)
            e["r"] = []

    def op(self, eng, build, reads=(), writes=()):
        idx = self.n[eng]
        cmd = Cmd(eng, idx, build)
        self._deps(cmd, eng, reads, writes)
        self.cmds[eng].append(cmd)
        self.n[eng] += 1
        self._record(eng, idx, reads, writes)
        return (eng, idx)

    def dma(self, queue, out, in_, reads=(), writes=()):
        issuer = Q_ISSUER[queue]
        qidx = self.n[queue]
        lane, ordinal = qidx % self.NLANE, qidx // self.NLANE

        def build(e, out=out, in_=in_):
            return e.dma_start(out, in_)
        cmd = Cmd(queue, qidx, build)
        cmd.idx = (lane, ordinal)
        self._deps(cmd, issuer, reads, writes)
        if ordinal > 0:
            self._qwait(cmd, issuer, queue, lane, ordinal * 16)
        self.cmds[issuer].append(cmd)
        self.n[queue] += 1
        self._record(queue, qidx, reads, writes)
        return (queue, qidx)

    def cc(self, build, reads=(), writes=()):
        qidx = self.n["qcc"]
        cmd = Cmd("qcc", qidx, build)
        cmd.idx = (0, qidx)
        self._deps(cmd, "pool", reads, writes)
        if qidx > 0:
            self._qwait(cmd, "pool", "qcc", 0, qidx)
        self.cmds["pool"].append(cmd)
        self.n["qcc"] += 1
        self._record("qcc", qidx, reads, writes)
        return ("qcc", qidx)

    def barrier(self, engines=ENGS):
        etargets = [(k, self.n[k] - 1) for k in ENGS if self.n[k] > 0]
        for e in engines:
            cmd = Cmd(e, None, None)
            for (k, i) in etargets:
                if k == e:
                    continue
                self._ewait(cmd, e, k, i)
            for q in QUEUES:
                nl = 1 if q == "qcc" else self.NLANE
                unit = 1 if q == "qcc" else 16
                for lane in range(nl):
                    cnt = (self.n[q] - lane + nl - 1) // nl
                    if cnt > 0:
                        self._qwait(cmd, e, q, lane, cnt * unit)
            if cmd.waits:
                self.cmds[e].append(cmd)

    def finish(self, block):
        rank = {}
        for k in ENGS:
            for i, m in enumerate(sorted(self.marks[k])):
                rank[(k, m)] = i + 1

        def runner(ename):
            def body(engine):
                for cmd in self.cmds[ename]:
                    for w in cmd.waits:
                        if w[0] == "e":
                            engine.wait_ge(self.tick[w[1]], rank[(w[1], w[2])])
                        else:
                            engine.wait_ge(self.qtick[w[1]][w[2]], w[3])
                    if cmd.build is None:
                        continue
                    inst = cmd.build(engine)
                    if cmd.key in ENGS:
                        if cmd.idx is not None and cmd.idx in self.marks[cmd.key]:
                            if inst is None:
                                inst = engine.engine_nop()
                            inst.then_inc(self.tick[cmd.key], 1)
                    else:
                        lane, _ = cmd.idx
                        unit = 1 if cmd.key == "qcc" else 16
                        inst.then_inc(self.qtick[cmd.key][lane], unit)
            return body
        block.tensor(runner("pe"))
        block.scalar(runner("act"))
        block.vector(runner("dve"))
        block.gpsimd(runner("pool"))
        block.sync(runner("sp"))



def _host_constants(core):
    c = {}
    u = np.triu(np.ones((CH, CH), np.float32))
    c["u_aug"] = np.ascontiguousarray(
        np.concatenate([u, np.ones((CH, 1), np.float32)], 1))
    c["l_strict"] = np.ascontiguousarray(
        np.tril(np.ones((128, 128), np.float32), -1))
    c["ones_f"] = np.ones((128, 128), np.float32)
    c["ident_f"] = np.eye(128, dtype=np.float32)
    c["ident_b"] = np.eye(128, dtype=ml_dtypes.bfloat16)
    dh = 64
    inv = 1.0 / (10000.0 ** (np.arange(0, dh, 2, dtype=np.float32) / dh))
    fr = np.outer(np.arange(T, dtype=np.float32), inv)
    emb = np.concatenate([fr, fr], -1)
    cosf = np.cos(emb).T.astype(np.float32)
    sinf = np.sin(emb).T.astype(np.float32)
    c["cos_f"] = np.ascontiguousarray(
        np.concatenate([cosf, cosf], 0).astype(ml_dtypes.bfloat16))
    c["sin_f"] = np.ascontiguousarray(
        np.concatenate([sinf, sinf], 0).astype(ml_dtypes.bfloat16))
    oh = np.zeros((1, 8), np.float32)
    oh[0, core] = 1.0
    c["onehot"] = oh
    mlt = np.zeros((8, 1), np.float32)
    mlt[:core, 0] = 1.0
    c["masklt_rep"] = np.ascontiguousarray(np.broadcast_to(mlt, (8, 128)))
    pm = np.zeros((128, 128), np.float32)
    for o in (0, 64):
        for r in range(32):
            pm[o + 32 + r, o + r] = -1.0
            pm[o + r, o + 32 + r] = 1.0
    c["permr"] = pm.astype(ml_dtypes.bfloat16)
    hs = np.zeros((16, 1024), np.float32)
    for p in range(8):
        hs[2 * p, p * 128:p * 128 + 64] = 1.0
        hs[2 * p + 1, p * 128 + 64:(p + 1) * 128] = 1.0
    c["hselb"] = hs.astype(ml_dtypes.bfloat16)
    return c


def make_in_maps(inputs):
    x = np.asarray(inputs["L0_attn_x"])
    wgk2b = np.concatenate([np.asarray(inputs["L0_attn_Wgk2"]),
                            np.asarray(inputs["L0_attn_bgk2"])[None, :]], 0)
    rw = np.asarray(inputs["L0_attn_rms_w"], dtype=np.float32)
    rmsw_f = np.ascontiguousarray(rw.reshape(2, 128).T)
    shared = {
        "wq": np.ascontiguousarray(inputs["L0_attn_Wq"]),
        "wk": np.ascontiguousarray(inputs["L0_attn_Wk"]),
        "wv": np.ascontiguousarray(inputs["L0_attn_Wv"]),
        "wg0": np.ascontiguousarray(inputs["L0_attn_Wg"]),
        "wgk1": np.ascontiguousarray(inputs["L0_attn_Wgk1"]),
        "wgk2b": np.ascontiguousarray(wgk2b),
        "wout": np.ascontiguousarray(inputs["L0_attn_Wout"]),
        "rmsw_f": rmsw_f,
        "wgt": np.ascontiguousarray(np.asarray(inputs["L0_ffn_w_gate"]).T),
        "wut": np.ascontiguousarray(np.asarray(inputs["L0_ffn_w_up"]).T),
        "wdt": np.ascontiguousarray(np.asarray(inputs["L0_ffn_w_down"]).T),
        "wqkv": np.ascontiguousarray(inputs["L1_attn_Wqkv"]),
        "wo": np.ascontiguousarray(inputs["L1_attn_Wo"]),
        "wgm": np.ascontiguousarray(inputs["L1_ffn_wg"]),
        "wet": np.ascontiguousarray(
            np.transpose(np.asarray(inputs["L1_ffn_W"]), (0, 2, 1))),
        "bm": np.ascontiguousarray(inputs["L1_ffn_b"]),
    }
    maps = []
    for core in range(8):
        m = dict(shared)
        m["x"] = np.ascontiguousarray(np.asarray(x[core], dtype=np.float32))
        m.update(_host_constants(core))
        maps.append(m)
    return maps


def col_slab(w, f0, fw):
    return w[:, f0:f0 + fw].rearrange("(kc p) f -> p kc f", p=128)



def build_program(taps=(), stages=("l0a", "l0f", "l1a", "l1f")):
    nc = bass.Bass("TRN2", target_bir_lowering=False)
    g = {}

    def din(name, shape, dtype=F32):
        g[name] = nc.dram_tensor(name, list(shape), dtype,
                                 kind="ExternalInput").ap()

    din("x", [T, D])
    din("wq", [D, 512]); din("wk", [D, 512]); din("wv", [D, D]); din("wg0", [D, D])
    din("wgk1", [D, 16]); din("wgk2b", [17, 512]); din("wout", [D, D])
    din("rmsw_f", [128, 2])
    din("wgt", [D, HID], I32); din("wut", [D, HID], I32); din("wdt", [HID, D], I32)
    din("wqkv", [D, 3 * D]); din("wo", [D, D])
    din("wgm", [D, EXP]); din("wet", [EXP, D, D]); din("bm", [EXP, D])
    din("u_aug", [CH, CH + 1]); din("l_strict", [128, 128]); din("ones_f", [128, 128])
    din("ident_f", [128, 128]); din("ident_b", [128, 128], BF16)
    din("cos_f", [128, T], BF16); din("sin_f", [128, T], BF16)
    din("onehot", [1, 8]); din("masklt_rep", [8, 128])
    din("permr", [128, 128], BF16); din("hselb", [16, 1024], BF16)

    out = nc.dram_tensor("out", [T, D], F32, kind="ExternalOutput").ap()
    tap_outs = {}
    for tname in taps:
        tap_outs[tname] = nc.dram_tensor("tap_" + tname, [T, D], F32,
                                         kind="ExternalOutput").ap()
    cc_in_t = nc.dram_tensor("cc_in", [8, 8], F32, kind="Internal")
    cc_out_t = nc.dram_tensor("cc_out", [8, 8], F32, kind="Internal")

    stack = ExitStack()
    with stack:
        def sb(name, shape, dt=F32):
            return stack.enter_context(nc.sbuf_tensor(name, shape, dt))

        P = Prog(nc, stack, same_engine_sync=SAME_ENGINE_SYNC)

        RES = sb("RES", [128, NT * D])
        XT = sb("XT", [128, ND * T], BF16)
        WB = sb("WB", [128, 8 * 1024], BF16)
        UAUG = sb("UAUG", [CH, CH + 1])
        LSTRF = sb("LSTRF", [128, 128])
        ONESF = sb("ONESF", [128, 128])
        IDF = sb("IDF", [128, 128])
        IDB = sb("IDB", [128, 128], BF16)
        RMSWF = sb("RMSWF", [128, 2])
        LNT = sb("LNT", [128, 2048])
        GARB = sb("GARB", [128, 512], BF16)
        LNC = sb("LNC", [128, 16])
        LNEPS = sb("LNEPS", [128, 1])
        ONE1 = sb("ONE1", [128, 1])

        def wslot(i):
            return WB[:, i * 1024:(i + 1) * 1024]

        def wtile(i, k, w=128):
            return WB[:, i * 1024 + k * w: i * 1024 + (k + 1) * w]

        def tap(name, reads):
            if name in tap_outs:
                P.dma("qsp",
                      tap_outs[name].rearrange("(i p) d -> p i d", p=128),
                      RES[:].rearrange("p (i d) -> p i d", d=D), reads=reads)

        for nm, dst in (("u_aug", UAUG), ("l_strict", LSTRF), ("ones_f", ONESF),
                        ("ident_f", IDF), ("ident_b", IDB), ("rmsw_f", RMSWF)):
            P.dma("qsp", dst[:], g[nm], writes=[nm])
        P.op("dve", lambda v: v.memset(LNEPS[:], EPS), writes=["lneps"])
        P.op("dve", lambda v: v.memset(ONE1[:], 1.0), writes=["one1"])
        for i in range(NT):
            P.dma("qsp", RES[:, i * D:(i + 1) * D],
                  g["x"][i * 128:(i + 1) * 128, :], writes=[("res", i)])

        res_r = [("res", i) for i in range(NT)]

        def xt_r(i=None):
            if i is None:
                return [("xt", k, j) for k in range(ND) for j in range(NT)]
            return [("xt", k, i) for k in range(ND)]

        def build_xt(tag):
            with ExitStack() as st:
                PT = [st.enter_context(nc.psum_tensor(f"ptx_{tag}{b}", [128, 512]))
                      for b in range(4)]
                idx = 0
                for i in range(NT):
                    for k in range(ND):
                        b = idx % 4
                        idx += 1
                        P.op("pe", lambda t, i=i, k=k, b=b: t.transpose(
                                PT[b][:, 0:128],
                                RES[:, i * D + k * 128: i * D + (k + 1) * 128],
                                IDF[:]),
                             reads=[("res", i), "ident_f"],
                             writes=[("ptx", b)])
                        P.op("dve", lambda v, i=i, k=k, b=b: v.tensor_copy(
                                XT[:, k * T + i * 128: k * T + (i + 1) * 128],
                                PT[b][:, 0:128]),
                             reads=[("ptx", b)], writes=[("xt", k, i)])

        def ln_apply(i, f0, f1, f_reads, keep_col=None, keep_reads=()):
            pr = i % 2
            LT = LNT[:, pr * 1024:(pr + 1) * 1024]
            LC = LNC[:, pr * 8:(pr + 1) * 8]
            def ck(j):
                return ("lnc", pr, j)
            kt0, kt1, ksq = ("lnt0", pr), ("lnt1", pr), ("lnsq", pr)
            kc = 1.0 if keep_col is None else keep_col
            P.op("dve", lambda v: v.scalar_tensor_tensor(
                    LT[:, 0:512], f0, kc, RES[:, i * D: i * D + 512],
                    ALU.mult, ALU.add, accum_out=LC[:, 0:1]),
                 reads=[("res", i), *f_reads, *keep_reads], writes=[kt0, ck(0)])
            P.op("dve", lambda v: v.scalar_tensor_tensor(
                    LT[:, 512:1024], f1, kc, RES[:, i * D + 512:(i + 1) * D],
                    ALU.mult, ALU.add, accum_out=LC[:, 1:2]),
                 reads=[("res", i), *f_reads, *keep_reads], writes=[kt1, ck(1)])
            P.op("dve", lambda v: v.scalar_tensor_tensor(
                    GARB[:], LT[:, 0:512], 1.0, LT[:, 0:512],
                    ALU.mult, ALU.mult, accum_out=LC[:, 2:3]),
                 reads=[kt0], writes=["garb", ck(2)])
            P.op("dve", lambda v: v.scalar_tensor_tensor(
                    GARB[:], LT[:, 512:1024], 1.0, LT[:, 512:1024],
                    ALU.mult, ALU.mult, accum_out=LC[:, 3:4]),
                 reads=[kt1], writes=["garb", ck(3)])
            P.op("dve", lambda v: v.tensor_add(LC[:, 4:5], LC[:, 0:1],
                                               LC[:, 1:2]),
                 reads=[ck(0), ck(1)], writes=[ck(4)])
            P.op("dve", lambda v: v.tensor_add(LC[:, 5:6], LC[:, 2:3],
                                               LC[:, 3:4]),
                 reads=[ck(2), ck(3)], writes=[ck(5)])
            P.op("dve", lambda v: v.tensor_scalar(
                    LC[:, 6:7], LC[:, 4:5], 1.0 / D, None, ALU.mult),
                 reads=[ck(4)], writes=[ck(6)])
            P.op("dve", lambda v: v.scalar_tensor_tensor(
                    LC[:, 7:8], LC[:, 6:7], LC[:, 4:5], LC[:, 5:6],
                    ALU.mult, ALU.subtract),
                 reads=[ck(6), ck(4), ck(5)], writes=[ck(7)])
            P.op("act", lambda s: s.activation(
                    LC[:, 7:8], LC[:, 7:8], AF.Ln, bias=LNEPS[:, 0:1],
                    scale=-1.0 / D),
                 reads=[ck(7), "lneps"], writes=[ck(7)])
            P.op("act", lambda s: s.activation(
                    LC[:, 7:8], LC[:, 7:8], AF.Exp, scale=-0.5),
                 reads=[ck(7)], writes=[ck(7)])
            P.op("dve", lambda v: v.scalar_tensor_tensor(
                    LC[:, 6:7], LC[:, 6:7], -1.0, LC[:, 7:8],
                    ALU.mult, ALU.mult),
                 reads=[ck(6), ck(7)], writes=[ck(6)])
            P.op("dve", lambda v: v.tensor_scalar(
                    RES[:, i * D: i * D + 512], LT[:, 0:512],
                    LC[:, 7:8], LC[:, 6:7], ALU.mult, ALU.add),
                 reads=[kt0, ck(6), ck(7)], writes=[("res", i)])
            P.op("dve", lambda v: v.tensor_scalar(
                    RES[:, i * D + 512:(i + 1) * D], LT[:, 512:1024],
                    LC[:, 7:8], LC[:, 6:7], ALU.mult, ALU.add),
                 reads=[kt1, ck(6), ck(7)], writes=[("res", i)])

        def stage_l0a():
            with ExitStack() as st:
                def lsb(n, s, d=F32):
                    return st.enter_context(nc.sbuf_tensor(n, s, d))

                def lps(n, s, d=F32):
                    return st.enter_context(nc.psum_tensor(n, s, d))
                PS = [lps(f"pa{i}", [128, 512]) for i in range(7)]
                st_scan = ExitStack()
                PSB = st_scan.enter_context(nc.psum_tensor("pab", [128, 1024], BF16))
                UT = lsb("UT", [32, T], BF16)
                GSA = lsb("GSA", [128, 8 * 129])
                GSB = lsb("GSB", [128, 8 * 129])
                AEND = lsb("AEND", [128, 16])
                GH = lsb("GH", [128, 1024])
                QT = lsb("QT", [128, 1024], BF16)
                KT = lsb("KT", [128, 1024], BF16)
                KTOK = lsb("KTOK", [128, 256], BF16)
                KH = lsb("KH", [128, 256], BF16)
                VT = lsb("VT", [128, 512], BF16)
                SST = [lsb("SST0", [128, 256]), lsb("SST1", [128, 256])]
                SSTB = lsb("SSTB", [128, 256], BF16)
                OBUF = lsb("OBUF", [128, 2 * T])
                OSQ = lsb("OSQ", [128, 1024], BF16)
                OSF = lsb("OSF", [128, 1024])
                OST = OSF[0:1, 0:512]
                INVR = lsb("INVR", [1, 512], BF16)
                ONEROWB = lsb("ONEROWB", [1, 128], BF16)
                ONECOLB = lsb("ONECOLB", [128, 1], BF16)
                GATE = lsb("GATE", [128, 1024], BF16)
                OPRIME = lsb("OPRIME", [128, 8 * T], BF16)
                WGK1 = lsb("WGK1", [128, 128], BF16)
                WGK2B = lsb("WGK2B", [17, 512], BF16)
                PTS = lsb("PTS", [128, 128], BF16)

                P.op("dve", lambda v: v.memset(ONEROWB[:], 1.0),
                     writes=["onerowb"])
                P.op("dve", lambda v: v.memset(ONECOLB[:], 1.0),
                     writes=["onecolb"])
                P.dma("qpool", WGK1[:].rearrange("p (kc f) -> p kc f", f=16),
                      col_slab(g["wgk1"], 0, 16), writes=["wgk1"])
                P.dma("qpool", WGK2B[:], g["wgk2b"], writes=["wgk2b"])
                P.op("dve", lambda v: v.memset(UT[:], 1.0), writes=["ut"])
                for tc in range(4):
                    for k in range(ND):
                        P.op("pe", lambda t, tc=tc, k=k: t.matmul(
                                PS[0][0:16, 0:512], WGK1[:, k * 16:(k + 1) * 16],
                                XT[:, k * T + tc * 512: k * T + (tc + 1) * 512],
                                start=(k == 0), stop=(k == ND - 1)),
                             reads=["wgk1", *xt_r()], writes=[("psb", 0)])
                    P.op("dve", lambda v, tc=tc: v.tensor_copy(
                            UT[0:16, tc * 512:(tc + 1) * 512], PS[0][0:16, 0:512]),
                         reads=[("psb", 0)], writes=["ut"])

                def prefetch_head2(h):
                    b = (h % 2) * 4
                    P.dma("qpool", wslot(b + 0).rearrange("p (kc f) -> p kc f", f=128),
                          col_slab(g["wq"], h * 128, 128),
                          writes=[("wb", b + 0)])
                    P.dma("qpool", wslot(b + 1).rearrange("p (kc f) -> p kc f", f=128),
                          col_slab(g["wk"], h * 128, 128),
                          writes=[("wb", b + 1)])
                    P.dma("qpool", WB[:, (b + 2) * 1024:(b + 4) * 1024]
                          .rearrange("p (kc f) -> p kc f", f=256),
                          col_slab(g["wv"], h * 256, 256),
                          writes=[("wb", b + 2), ("wb", b + 3)])

                prefetch_head2(0)
                for h in range(4):
                    b = (h % 2) * 4
                    if h < 3:
                        prefetch_head2(h + 1)
                    P.op("dve", lambda v: v.memset(SST[0][:], 0.0),
                         writes=["sst0"])
                    P.op("dve", lambda v: v.memset(SSTB[:], 0.0),
                         writes=["sstb"])
                    for half in range(2):
                        for cc in range(8):
                            c = half * 8 + cc
                            zb = cc % 2
                            P.op("pe", lambda t, c=c, h=h, zb=zb: t.matmul(
                                    PS[zb][:, 0:128],
                                    UT[0:17, c * 128:(c + 1) * 128],
                                    WGK2B[0:17, h * 128:(h + 1) * 128],
                                    start=True, stop=True),
                                 reads=["ut", "wgk2b"], writes=[("psb", zb)])
                            P.op("dve", lambda v, cc=cc, zb=zb: v.tensor_copy(
                                    GSA[:, cc * 128:(cc + 1) * 128],
                                    PS[zb][:, 0:128]),
                                 reads=[("psb", zb)], writes=["gsa"])
                        ZV = GSA[:, 0:1024]
                        LV = GSB[:, 0:1024]
                        P.op("dve", lambda v: v.tensor_scalar_min(LV, ZV, 0.0),
                             reads=["gsa"], writes=["gsb"])
                        P.op("act", lambda s: s.activation(ZV, ZV, AF.Abs),
                             reads=["gsa"], writes=["gsa"])
                        P.op("act", lambda s: s.activation(ZV, ZV, AF.Exp,
                                                           scale=-1.0),
                             reads=["gsa"], writes=["gsa"])
                        P.op("act", lambda s: s.activation(ZV, ZV, AF.Ln,
                                                           bias=ONE1[:, 0:1]),
                             reads=["gsa", "one1"], writes=["gsa"])
                        P.op("dve", lambda v: v.tensor_sub(GH[:], LV, ZV),
                             reads=["gsa", "gsb"], writes=["gh"])
                        P.op("dve", lambda v: v.tensor_scalar(
                                GH[:], GH[:], 1.0 / 16.0, -3.0, ALU.mult, ALU.max),
                             reads=["gh"], writes=["gh"])
                        for cc in range(8):
                            zb = cc % 2
                            P.op("pe", lambda t, cc=cc, zb=zb: t.matmul(
                                    PS[zb][:, 0:129],
                                    GH[:, cc * 128:(cc + 1) * 128], UAUG[:],
                                    start=True, stop=True),
                                 reads=["gh", "u_aug"], writes=[("psb", zb)])
                            P.op("act", lambda s, cc=cc, zb=zb: s.activation(
                                    GSA[:, cc * 128:(cc + 1) * 128],
                                    PS[zb][:, 0:128], AF.Exp),
                                 reads=[("psb", zb)], writes=["gsa"])
                            P.op("act", lambda s, cc=cc, zb=zb: s.activation(
                                    AEND[:, cc:cc + 1],
                                    PS[zb][:, 128:129], AF.Exp),
                                 reads=[("psb", zb)], writes=["aend"])
                            P.op("act", lambda s, cc=cc, zb=zb: s.activation(
                                    GSB[:, cc * 128:(cc + 1) * 128],
                                    PS[zb][:, 0:128], AF.Exp, scale=-1.0),
                                 reads=[("psb", zb)], writes=["gsb"])
                        for tq in range(2):
                            tc = half * 2 + tq
                            ro = tq * 512
                            for k in range(ND):
                                for (jj, slot) in ((2, b + 0), (3, b + 1)):
                                    P.op("pe", lambda t, slot=slot, k=k, tc=tc,
                                         jj=jj: t.matmul(
                                            PS[jj][:, 0:512], wtile(slot, k),
                                            XT[:, k * T + tc * 512:
                                               k * T + (tc + 1) * 512],
                                            start=(k == 0), stop=(k == ND - 1)),
                                         reads=[("wb", slot), *xt_r()],
                                         writes=[("psb", jj)])
                            for (jj, dstbuf, dkey, sckey) in (
                                    (2, QT, "qt", "gsa"), (3, KT, "kt", "gsb")):
                                scalebuf = GSA if sckey == "gsa" else GSB
                                P.op("dve", lambda v, ro=ro, tq=tq, jj=jj,
                                     dstbuf=dstbuf, scalebuf=scalebuf:
                                     v.tensor_mul(
                                        dstbuf[:, ro:ro + 512],
                                        PS[jj][:, 0:512],
                                        scalebuf[:, tq * 512:(tq + 1) * 512]),
                                     reads=[("psb", jj), sckey], writes=[dkey])
                            for c4 in range(4):
                                c = tc * 4 + c4
                                lc = tq * 4 + c4
                                ro128 = ro + c4 * 128
                                vr = (c % 2) * 256
                                kr = (c % 2) * 128
                                pbank = 4 if c % 2 == 0 else 0
                                kvbank = 6 if c % 2 == 0 else 1
                                obank = 5 if c % 2 == 0 else 2
                                cur, nxt = SST[c % 2], SST[(c + 1) % 2]
                                ck, nk = f"sst{c % 2}", f"sst{(c + 1) % 2}"
                                for k in range(ND):
                                    P.op("pe", lambda t, k=k, c=c, b=b: t.matmul(
                                            PS[3][:, 0:256],
                                            XT[:, k * T + c * 128:
                                               k * T + (c + 1) * 128],
                                            WB[:, (b + 2) * 1024 + k * 256:
                                               (b + 2) * 1024 + (k + 1) * 256],
                                            start=(k == 0), stop=(k == ND - 1)),
                                         reads=[("wb", b + 2), ("wb", b + 3),
                                                *xt_r()],
                                         writes=[("psb", 3)])
                                P.op("act", lambda a, vr=vr: a.copy(
                                        VT[:, vr:vr + 256], PS[3][:, 0:256]),
                                     reads=[("psb", 3)], writes=["vt"])
                                P.op("dve", lambda v, ro128=ro128, kr=kr, lc=lc:
                                     v.tensor_scalar(
                                        KH[:, kr:kr + 128],
                                        KT[:, ro128:ro128 + 128],
                                        AEND[:, lc:lc + 1], None, ALU.mult),
                                     reads=["kt", "aend"], writes=["kh"])
                                P.op("pe", lambda t, kr=kr: t.transpose(
                                        PSB[:, 0:128], KH[:, kr:kr + 128],
                                        IDB[:]),
                                     reads=["kh", "ident_b"], writes=[("psb", 7)])
                                P.op("dve", lambda v, kr=kr: v.tensor_copy(
                                        KTOK[:, kr:kr + 128], PSB[:, 0:128]),
                                     reads=[("psb", 7)], writes=["ktok"])
                                P.op("pe", lambda t, ro128=ro128, pbank=pbank:
                                     t.matmul(
                                        PS[pbank][:, 0:128],
                                        KT[:, ro128:ro128 + 128],
                                        QT[:, ro128:ro128 + 128],
                                        start=True, stop=True),
                                     reads=["kt", "qt"], writes=[("psb", pbank)])
                                P.op("dve", lambda v, pbank=pbank: v.tensor_mul(
                                        PTS[:], PS[pbank][:, 0:128],
                                        UAUG[:, 0:128]),
                                     reads=[("psb", pbank), "u_aug"],
                                     writes=["pts"])
                                for blk in range(2):
                                    P.op("pe", lambda t, blk=blk, obank=obank,
                                         ro128=ro128: t.matmul(
                                            PS[obank][:, blk * 128:(blk + 1) * 128],
                                            SSTB[:, blk * 128:(blk + 1) * 128],
                                            QT[:, ro128:ro128 + 128],
                                            start=(blk == 0), stop=False),
                                         reads=["sstb", "qt"],
                                         writes=[("psb", obank)])
                                for blk in range(2):
                                    P.op("pe", lambda t, blk=blk, vr=vr,
                                         obank=obank: t.matmul(
                                            PS[obank][:, blk * 128:(blk + 1) * 128],
                                            VT[:, vr + blk * 128:
                                               vr + (blk + 1) * 128],
                                            PTS[:], start=False, stop=(blk == 1)),
                                         reads=["vt", "pts"],
                                         writes=[("psb", obank)])
                                P.op("pe", lambda t, kr=kr, vr=vr,
                                     kvbank=kvbank: t.matmul(
                                        PS[kvbank][:, 0:256], KTOK[:, kr:kr + 128],
                                        VT[:, vr:vr + 256], start=True, stop=True),
                                     reads=["ktok", "vt"],
                                     writes=[("psb", kvbank)])
                                P.op("dve", lambda v, cur=cur, lc=lc,
                                     kvbank=kvbank: v.scalar_tensor_tensor(
                                        SSTB[:], cur[:], AEND[:, lc:lc + 1],
                                        PS[kvbank][:, 0:256],
                                        ALU.mult, ALU.add),
                                     reads=[ck, "aend", ("psb", kvbank)],
                                     writes=["sstb"])
                                P.op("dve", lambda v, cur=cur, nxt=nxt, lc=lc,
                                     kvbank=kvbank: v.scalar_tensor_tensor(
                                        nxt[:], cur[:], AEND[:, lc:lc + 1],
                                        PS[kvbank][:, 0:256],
                                        ALU.mult, ALU.add),
                                     reads=[ck, "aend", ("psb", kvbank)],
                                     writes=[nk])
                                P.op("act", lambda a, c=c, obank=obank:
                                     a.copy(
                                        OBUF[:, c * 256:(c + 1) * 256],
                                        PS[obank][:, 0:256]),
                                     reads=[("psb", obank)], writes=["obuf"])

                    OBV = OBUF.rearrange("q (c two f) -> q c two f", two=2, f=128)
                    for blk in range(2):
                        for grp in range(4):
                            par = (blk * 4 + grp) % 2
                            osq = OSQ[:, par * 512:(par + 1) * 512]
                            P.op("act", lambda s, blk=blk, grp=grp, osq=osq:
                                 s.activation(
                                    osq, OBV[:, grp * 4:(grp + 1) * 4, blk, :],
                                    AF.Square),
                                 reads=["obuf"], writes=[("osq", par)])
                            P.op("pe", lambda t, grp=grp, blk=blk, osq=osq:
                                 t.matmul(
                                    PS[grp][0:1, 0:512], ONECOLB[:], osq,
                                    start=(blk == 0), stop=(blk == 1)),
                                 reads=[("osq", par), "onecolb"],
                                 writes=[("psb", grp)])
                    for blk in range(2):
                        P.dma("qpool", wslot(b + blk)
                              .rearrange("p (kc f) -> p kc f", f=128),
                              col_slab(g["wg0"], h * 256 + blk * 128, 128),
                              writes=[("wb", b + blk)])
                    for grp in range(4):
                        P.op("act", lambda s, grp=grp: s.activation(
                                OST,
                                PS[grp][0:1, 0:512], AF.Ln,
                                bias=LNEPS[0:1, 0:1], scale=1.0 / 256.0),
                             reads=[("psb", grp), "lneps"],
                             writes=["ost", ("osf", 0)])
                        P.op("act", lambda s: s.activation(
                                INVR[0:1, :], OST, AF.Exp,
                                scale=-0.5),
                             reads=["ost", ("osf", 0)], writes=["invr"])
                        P.op("pe", lambda t, grp=grp: t.matmul(
                                PS[grp][:, 0:512], ONEROWB[0:1, :],
                                INVR[0:1, 0:512], start=True, stop=True),
                             reads=["onerowb", "invr"], writes=[("psb", grp)])
                    for g2 in range(2):
                        for blk in range(2):
                            fb = h * 2 + blk
                            gslot = b + blk
                            for k in range(ND):
                                for gi in range(2):
                                    grp = 2 * g2 + gi
                                    P.op("pe", lambda t, k=k, grp=grp,
                                         gslot=gslot, gi=gi: t.matmul(
                                            PS[4 + gi][:, 0:512],
                                            wtile(gslot, k),
                                            XT[:, k * T + grp * 512:
                                               k * T + (grp + 1) * 512],
                                            start=(k == 0),
                                            stop=(k == ND - 1)),
                                         reads=[("wb", gslot), *xt_r()],
                                         writes=[("psb", 4 + gi)])
                            for gi in range(2):
                                grp = 2 * g2 + gi
                                par = gi
                                gate = GATE[:, par * 512:(par + 1) * 512]
                                osf = OSF[:, par * 512:(par + 1) * 512]
                                P.op("act", lambda s, gate=gate, gi=gi:
                                     s.activation(gate, PS[4 + gi][:, 0:512],
                                                  AF.Silu),
                                     reads=[("psb", 4 + gi)],
                                     writes=[("gate", par)])
                                P.op("dve", lambda v, blk=blk, grp=grp, osf=osf:
                                     v.scalar_tensor_tensor(
                                        osf,
                                        OBV[:, grp * 4:(grp + 1) * 4, blk, :],
                                        RMSWF[:, blk:blk + 1],
                                        PS[grp][:, 0:512],
                                        ALU.mult, ALU.mult),
                                     reads=["obuf", "rmsw_f", ("psb", grp)],
                                     writes=[("osf", par)])
                                P.op("dve", lambda v, fb=fb, grp=grp, osf=osf,
                                     gate=gate: v.tensor_mul(
                                        OPRIME[:, fb * T + grp * 512:
                                               fb * T + (grp + 1) * 512],
                                        osf, gate),
                                     reads=[("osf", par), ("gate", par)],
                                     writes=[("opr", fb)])

                st_scan.close()
                PS.append(st.enter_context(nc.psum_tensor("pa7", [128, 512])))
                for tg in range(4):
                    for kk in range(ND):
                        slot = 6 + (kk % 2)
                        P.dma("qpool", wslot(slot),
                              g["wout"][kk * 128:(kk + 1) * 128, :],
                              writes=[("wb", slot)])
                        for it in range(4):
                            i = tg * 4 + it
                            for dh2 in range(2):
                                bank = it * 2 + dh2
                                P.op("pe", lambda t, kk=kk, i=i, dh2=dh2,
                                     bank=bank, slot=slot: t.matmul(
                                        PS[bank][:, 0:512],
                                        OPRIME[:, kk * T + i * 128:
                                               kk * T + (i + 1) * 128],
                                        wslot(slot)[:, dh2 * 512:(dh2 + 1) * 512],
                                        start=(kk == 0), stop=(kk == ND - 1)),
                                     reads=[("opr", kk), ("wb", slot)],
                                     writes=[("psb", bank)])
                    for it in range(4):
                        i = tg * 4 + it
                        ln_apply(i, PS[it * 2][:, 0:512],
                                 PS[it * 2 + 1][:, 0:512],
                                 f_reads=[("psb", it * 2), ("psb", it * 2 + 1)])

        def stage_l0f():
            with ExitStack() as st:
                def lsb(n, s, d=F32):
                    return st.enter_context(nc.sbuf_tensor(n, s, d))

                def lps(n, s, d=F32):
                    return st.enter_context(nc.psum_tensor(n, s, d))
                PS = [lps(f"pf{i}", [128, 512]) for i in range(8)]
                HBUF = lsb("HBUF", [128, NSLAB * 1024], BF16)
                SIL = lsb("SIL", [128, 1024])
                DACC = lsb("DACC", [128, 8 * 512])
                for i in range(NT):
                    for k in range(ND):
                        tb_ = (i * ND + k) % 2
                        P.op("pe", lambda t, i=i, k=k, tb_=tb_: t.transpose(
                                PS[tb_][:, 0:128],
                                RES[:, i * D + k * 128: i * D + (k + 1) * 128],
                                IDF[:]),
                             reads=[("res", i), "ident_f"],
                             writes=[("psb", tb_)])
                        P.op("dve", lambda v, i=i, k=k, tb_=tb_: v.tensor_copy(
                                XT[:, k * T + i * 128: k * T + (i + 1) * 128],
                                PS[tb_][:, 0:128]),
                             reads=[("psb", tb_)], writes=[("xt", k, i)])
                for half in range(2):
                    t0 = half * 1024
                    for sg in range(12):
                        par = (sg % 2) * 4
                        P.dma("qpool",
                              WB[:, par * 1024:(par + 2) * 1024]
                              .rearrange("p (kc f) -> p kc f", f=256),
                              col_slab(g["wgt"], sg * 256, 256),
                              writes=[("wb", par), ("wb", par + 1)])
                        P.dma("qpool",
                              WB[:, (par + 2) * 1024:(par + 4) * 1024]
                              .rearrange("p (kc f) -> p kc f", f=256),
                              col_slab(g["wut"], sg * 256, 256),
                              writes=[("wb", par + 2), ("wb", par + 3)])
                        for s4 in range(2):
                            s_ = sg * 2 + s4
                            for k in range(ND):
                                for tc2 in range(2):
                                    t1 = t0 + tc2 * 512
                                    P.op("pe", lambda t, k=k, t1=t1, par=par,
                                         s4=s4, tc2=tc2: t.matmul(
                                            PS[4 + tc2][:, 0:512],
                                            WB[:, par * 1024 + (k * 2 + s4) * 128:
                                               par * 1024 + (k * 2 + s4 + 1) * 128],
                                            XT[:, k * T + t1: k * T + t1 + 512],
                                            start=(k == 0), stop=(k == ND - 1)),
                                         reads=[("wb", par), ("wb", par + 1),
                                                *[("xt", k, t1 // 128 + ii)
                                                  for ii in range(4)]],
                                         writes=[("psb", 4 + tc2)])
                            for k in range(ND):
                                for tc2 in range(2):
                                    t1 = t0 + tc2 * 512
                                    P.op("pe", lambda t, k=k, t1=t1, par=par,
                                         s4=s4, tc2=tc2: t.matmul(
                                            PS[6 + tc2][:, 0:512],
                                            WB[:, (par + 2) * 1024 +
                                               (k * 2 + s4) * 128:
                                               (par + 2) * 1024 +
                                               (k * 2 + s4 + 1) * 128],
                                            XT[:, k * T + t1: k * T + t1 + 512],
                                            start=(k == 0), stop=(k == ND - 1)),
                                         reads=[("wb", par + 2), ("wb", par + 3),
                                                *[("xt", k, t1 // 128 + ii)
                                                  for ii in range(4)]],
                                         writes=[("psb", 6 + tc2)])
                            for tc2 in range(2):
                                sil = SIL[:, tc2 * 512:(tc2 + 1) * 512]
                                P.op("act", lambda s, sil=sil, tc2=tc2:
                                     s.activation(sil, PS[4 + tc2][:, 0:512],
                                                  AF.Silu, scale=1.0 / 127.0),
                                     reads=[("psb", 4 + tc2)],
                                     writes=[("sil", tc2)])
                                P.op("dve", lambda v, s_=s_, tc2=tc2, sil=sil:
                                     v.scalar_tensor_tensor(
                                        HBUF[:, s_ * 1024 + tc2 * 512:
                                             s_ * 1024 + (tc2 + 1) * 512],
                                        PS[6 + tc2][:, 0:512], 1.0 / 127.0, sil,
                                        ALU.mult, ALU.mult),
                                     reads=[("psb", 6 + tc2), ("sil", tc2)],
                                     writes=[("hb", s_, tc2)])
                    for dh2 in range(2):
                        for sg in range(6):
                            par = (sg % 2) * 2
                            for hc in range(2):
                                P.dma("qpool",
                                      WB[:, (par + hc) * 1024:
                                         (par + hc + 1) * 1024]
                                      .rearrange("p (kc f) -> p kc f", f=512),
                                      g["wdt"][sg * 512 + hc * 256:
                                               sg * 512 + (hc + 1) * 256,
                                               dh2 * 512:(dh2 + 1) * 512]
                                      .rearrange("(kc p) f -> p kc f", p=128),
                                      writes=[("wb", par + hc)])
                            for s4 in range(4):
                                s_ = sg * 4 + s4
                                for it in range(8):
                                    P.op("pe", lambda t, s_=s_, it=it,
                                         par=par, s4=s4: t.matmul(
                                            PS[it][:, 0:512],
                                            HBUF[:, s_ * 1024 + it * 128:
                                                 s_ * 1024 + (it + 1) * 128],
                                            WB[:, par * 1024 + s4 * 512:
                                               par * 1024 + (s4 + 1) * 512],
                                            start=(s_ == 0),
                                            stop=(s_ == NSLAB - 1)),
                                         reads=[("hb", s_, it // 4),
                                                ("wb", par), ("wb", par + 1)],
                                         writes=[("psb", it)])
                        if dh2 == 0:
                            for it in range(8):
                                P.op("dve", lambda v, it=it: v.tensor_copy(
                                        DACC[:, it * 512:(it + 1) * 512],
                                        PS[it][:, 0:512]),
                                     reads=[("psb", it)],
                                     writes=[("dacc", it)])
                    for it in range(8):
                        i = half * 8 + it
                        ln_apply(i, DACC[:, it * 512:(it + 1) * 512],
                                 PS[it][:, 0:512],
                                 f_reads=[("dacc", it), ("psb", it)])
        def stage_l1a():
            with ExitStack() as st:
                def lsb(n, s, d=F32):
                    return st.enter_context(nc.sbuf_tensor(n, s, d))

                def lps(n, s, d=F32):
                    return st.enter_context(nc.psum_tensor(n, s, d))
                SPa = lps("SPa", [128, 1024])
                SPb = lps("SPb", [128, 1024])
                PO = lps("PO", [128, 1024])
                PBC = lps("PBC", [128, 512])
                SPS = lps("SPS", [128, 512])
                _BK = [(SPa, 0), (SPa, 512), (SPb, 0), (SPb, 512),
                       (PO, 0), (PO, 512), (PBC, 0), (SPS, 0)]

                def bk(j, lo=0, hi=512, p0=0, p1=128):
                    tns, off = _BK[j]
                    return tns[p0:p1, off + lo: off + hi]
                CQD = WB[:, 2 * 1024:3 * 1024]
                SQD = WB[:, 3 * 1024:4 * 1024]
                QTMP = WB[:, 4 * 1024:5 * 1024]
                RECA = WB[0:16, 5 * 1024:5 * 1024 + 512]
                QF = lsb("QF", [128, 1024], BF16)
                ATT = lsb("ATT", [128, 8 * 512], BF16)
                PTB = lsb("PTB", [128, 2048], BF16)
                KF = lsb("KF", [128, 8 * T], BF16)
                VG = lsb("VG", [128, NT * 16 * 65], BF16)
                vones = VG.rearrange("q (i h f) -> q i h f", h=16, f=65)
                TB = lsb("TBr", [128, 512], BF16)
                DSTG = lsb("DSTG", [1, 512], BF16)
                DENA = lsb("DENA", [16, 512], BF16)
                HSEL = lsb("HSEL", [16, 1024], BF16)
                PERMB = lsb("PERMB", [128, 128], BF16)

                P.dma("qpool", HSEL[:], g["hselb"], writes=["hsel"])
                P.dma("qpool", PERMB[:], g["permr"], writes=["permb"])

                def load_cs(par, c0):
                    P.dma("qpool", CQD[:, par * 512:(par + 1) * 512],
                          g["cos_f"][:, c0:c0 + 512],
                          writes=[("cq", par), ("wb", 2)])
                    P.dma("qpool", SQD[:, par * 512:(par + 1) * 512],
                          g["sin_f"][:, c0:c0 + 512],
                          writes=[("sq", par), ("wb", 3)])

                def load_pair(slot, base, p):
                    dst = WB[:, slot * 1024:(slot + 1) * 1024]
                    dv = dst.rearrange("q (kc two f) -> q kc two f", two=2, f=64)
                    P.dma("qpool", dv[:, :, 0, :],
                          col_slab(g["wqkv"], base + p * 64, 64),
                          writes=[("wb", slot)])
                    P.dma("qpool", dv[:, :, 1, :],
                          col_slab(g["wqkv"], base + (p + 8) * 64, 64),
                          writes=[("wb", slot)])

                def rope_combine(dst_ap, dkey, par, pb, cpar):
                    co = CQD[:, cpar * 512:(cpar + 1) * 512]
                    si = SQD[:, cpar * 512:(cpar + 1) * 512]
                    qt = QTMP[:, par * 512:(par + 1) * 512]
                    P.op("dve", lambda v: v.tensor_mul(qt, qt, co),
                         reads=[("qtmp", par), ("cq", cpar)],
                         writes=[("qtmp", par)])
                    P.op("dve", lambda v: v.tensor_mul(TB[:], bk(pb), si),
                         reads=[("psb", pb), ("sq", cpar)],
                         writes=["tbr"])
                    P.op("dve", lambda v: v.tensor_add(dst_ap, qt, TB[:]),
                         reads=[("qtmp", par), "tbr"], writes=[dkey])

                def evac_rot(par, src_bank, rot_bank):
                    qt = QTMP[:, par * 512:(par + 1) * 512]
                    P.op("dve", lambda v: v.tensor_copy(qt, bk(src_bank)),
                         reads=[("psb", src_bank)],
                         writes=[("qtmp", par), ("wb", 4)])
                    P.op("pe", lambda t: t.matmul(
                            bk(rot_bank), PERMB[:], qt, start=True, stop=True),
                         reads=["permb", ("qtmp", par)],
                         writes=[("psb", rot_bank)])

                for i in range(NT):
                    for k in range(ND):
                        tb_ = 6 + ((i * ND + k) % 2)
                        P.op("pe", lambda t, i=i, k=k, tb_=tb_: t.transpose(
                                bk(tb_, 0, 128),
                                RES[:, i * D + k * 128: i * D + (k + 1) * 128],
                                IDF[:]),
                             reads=[("res", i), "ident_f"],
                             writes=[("psb", tb_)])
                        P.op("dve", lambda v, i=i, k=k, tb_=tb_: v.tensor_copy(
                                XT[:, k * T + i * 128: k * T + (i + 1) * 128],
                                bk(tb_, 0, 128)),
                             reads=[("psb", tb_)], writes=[("xt", k, i)])

                load_pair(0, D, 0)
                for p in range(8):
                    sl = p % 2
                    for th in range(2):
                        tcA, tcB = 2 * th, 2 * th + 1
                        load_cs(0, tcA * 512)
                        load_cs(1, tcB * 512)
                        if th == 0 and p + 1 < 8:
                            load_pair((p + 1) % 2, D, p + 1)
                        for k in range(ND):
                            for (tc, jn) in ((tcA, 0), (tcB, 2)):
                                P.op("pe", lambda t, k=k, tc=tc, sl=sl, jn=jn:
                                     t.matmul(
                                        bk(jn), wtile(sl, k),
                                        XT[:, k * T + tc * 512:
                                           k * T + (tc + 1) * 512],
                                        start=(k == 0), stop=(k == ND - 1)),
                                     reads=[("wb", sl),
                                            *[("xt", k, tc * 4 + ii)
                                              for ii in range(4)]],
                                     writes=[("psb", jn)])
                        for (tc, par, jn) in ((tcA, 0, 0), (tcB, 1, 2)):
                            evac_rot(par, jn, jn + 1)
                            rope_combine(
                                KF[:, p * T + tc * 512:
                                   p * T + (tc + 1) * 512],
                                ("kf", p), par, jn + 1, par)

                P.op("dve", lambda v: v.memset(vones[:, :, :, 64:65], 1.0),
                     writes=["vg"])
                for vh in range(2):
                    for k in range(ND):
                        sl = vh * 4 + k % 4
                        extra = ()
                        if sl == 2:
                            extra = (("cq", 0), ("cq", 1))
                        elif sl == 3:
                            extra = (("sq", 0), ("sq", 1))
                        elif sl == 4:
                            extra = (("qtmp", k // 4),)
                        elif sl == 5 and k // 4 == 0:
                            extra = ("reca",)
                        P.dma("qpool",
                              WB[:, sl * 1024 + (k // 4) * 512:
                                 sl * 1024 + (k // 4) * 512 + 512],
                              g["wqkv"][k * 128:(k + 1) * 128,
                                        2 * D + vh * 512: 2 * D + (vh + 1) * 512],
                              writes=[("wb", sl), *extra])
                for i in range(NT):
                    for k in range(ND):
                        for vh in range(2):
                            sl = vh * 4 + k % 4
                            vb = 2 + vh + 2 * (i % 2)
                            P.op("pe", lambda t, k=k, i=i, sl=sl, vb=vb:
                                 t.matmul(
                                    bk(vb),
                                    XT[:, k * T + i * 128: k * T + (i + 1) * 128],
                                    WB[:, sl * 1024 + (k // 4) * 512:
                                       sl * 1024 + (k // 4) * 512 + 512],
                                    start=(k == 0), stop=(k == ND - 1)),
                                 reads=[("wb", sl), ("xt", k, i)],
                                 writes=[("psb", vb)])
                    for vh in range(2):
                        vb = 2 + vh + 2 * (i % 2)
                        hv = vones[:, i, vh * 8:(vh + 1) * 8, 0:64]
                        P.op("dve", lambda v, hv=hv, vb=vb: v.tensor_copy(
                                hv, bk(vb).rearrange(
                                    "q (h f) -> q h f", f=64)),
                             reads=[("psb", vb)], writes=["vg"])

                load_cs(0, 0)

                def q_proj(qc, p, ks):
                    sl = p % 2
                    for k in ks:
                        P.op("pe", lambda t, k=k, qc=qc, sl=sl: t.matmul(
                                bk(6), wtile(sl, k),
                                XT[:, k * T + qc * 512:
                                   k * T + (qc + 1) * 512],
                                start=(k == 0), stop=(k == ND - 1)),
                             reads=[("wb", sl),
                                    *[("xt", k, qc * 4 + ii)
                                      for ii in range(4)]],
                             writes=[("psb", 6)])

                def q_finish(qc, p):
                    par = p % 2
                    evac_rot(par, 6, 7)
                    rope_combine(QF[:, par * 512:(par + 1) * 512],
                                 ("qf", par), par, 7, qc % 2)

                for qc in range(4):
                    if qc == 0:
                        load_pair(0, 0, 0)
                        q_proj(0, 0, range(ND))
                        q_finish(0, 0)
                    for p in range(8):
                        qro = (p % 2) * 512
                        def emit_s(kv):
                            pp = kv % 2
                            jA = 2 * pp
                            stens = SPa if pp == 0 else SPb
                            po = pp * 1024
                            P.op("pe", lambda t, p=p, kv=kv, stens=stens,
                                 qro=qro: t.matmul(
                                    stens[:, 0:512],
                                    KF[0:64, p * T + kv * 128:
                                       p * T + (kv + 1) * 128],
                                    QF[0:64, qro:qro + 512],
                                    start=True, stop=True),
                                 reads=[("kf", p), ("qf", p % 2)],
                                 writes=[("psb", jA)])
                            P.op("pe", lambda t, p=p, kv=kv, stens=stens,
                                 qro=qro: t.matmul(
                                    stens[:, 512:1024],
                                    KF[64:128, p * T + kv * 128:
                                       p * T + (kv + 1) * 128],
                                    QF[64:128, qro:qro + 512],
                                    start=True, stop=True),
                                 reads=[("kf", p), ("qf", p % 2)],
                                 writes=[("psb", jA + 1)])
                            P.op("act", lambda s, stens=stens, po=po:
                                 s.activation(
                                    PTB[:, po:po + 512], stens[:, 0:512],
                                    AF.Exp, scale=0.125),
                                 reads=[("psb", jA)],
                                 writes=[("ptb", pp, 0)])
                            P.op("act", lambda s, stens=stens, po=po:
                                 s.activation(
                                    PTB[:, po + 512:po + 1024],
                                    stens[:, 512:1024],
                                    AF.Exp, scale=0.125),
                                 reads=[("psb", jA + 1)],
                                 writes=[("ptb", pp, 1)])

                        def emit_pv(kv):
                            pp = kv % 2
                            po = pp * 1024
                            P.op("pe", lambda t, kv=kv, p=p, po=po: t.matmul(
                                    PO[0:65, 0:512], vones[:, kv, p, :],
                                    PTB[:, po:po + 512],
                                    start=(kv == 0), stop=(kv == NT - 1)),
                                 reads=["vg", ("ptb", pp, 0)],
                                 writes=[("psb", 4)])
                            P.op("pe", lambda t, kv=kv, p=p, po=po: t.matmul(
                                    PO[0:65, 512:1024], vones[:, kv, p + 8, :],
                                    PTB[:, po + 512:po + 1024],
                                    start=(kv == 0), stop=(kv == NT - 1)),
                                 reads=["vg", ("ptb", pp, 1)],
                                 writes=[("psb", 5)])

                        emit_s(0)
                        for kv in range(NT):
                            if kv + 1 < NT:
                                emit_s(kv + 1)
                            if kv == 0 and p + 1 < 8:
                                load_pair((p + 1) % 2, 0, p + 1)
                            if kv == 0 and p == 7 and qc < 3:
                                load_pair(0, 0, 0)
                                load_cs((qc + 1) % 2, (qc + 1) * 512)
                            if 5 <= kv < 13 and p + 1 < 8:
                                q_proj(qc, p + 1, [kv - 5])
                            if kv == 13 and p + 1 < 8:
                                q_finish(qc, p + 1)
                            emit_pv(kv)
                        for hh in range(2):
                            h = p + 8 * hh
                            oj = 4 + hh
                            hr = (h % 2) * 64
                            hb = (h // 2) * 512
                            P.op("dve", lambda v, hr=hr, hb=hb, oj=oj:
                                 v.tensor_copy(
                                    ATT[hr:hr + 64, hb:hb + 512],
                                    bk(oj, p0=0, p1=64)),
                                 reads=[("psb", oj)], writes=[("att", h)])
                        P.op("dve", lambda v: v.tensor_copy(
                                DSTG[0:1, :], PO[64:65, 0:512]),
                             reads=[("psb", 4)], writes=["dstg"])
                        P.dma("qsp", DENA[2 * p:2 * p + 1, :], DSTG[0:1, :],
                              reads=["dstg"], writes=[("den", 2 * p)])
                        P.op("dve", lambda v: v.tensor_copy(
                                DSTG[0:1, :], PO[64:65, 512:1024]),
                             reads=[("psb", 5)], writes=["dstg"])
                        P.dma("qsp", DENA[2 * p + 1:2 * p + 2, :], DSTG[0:1, :],
                              reads=["dstg"], writes=[("den", 2 * p + 1)])

                    def _recip(v):
                        with nc.allow_low_precision(
                                reason="bf16 softmax denom broadcast"):
                            return v.reciprocal(RECA, DENA[:])
                    P.op("dve", _recip,
                         reads=[("den", r) for r in range(16)],
                         writes=["reca", ("wb", 5)])
                    for p in range(8):
                        nb = 6 + (p % 2)
                        P.op("pe", lambda t, p=p, nb=nb: t.matmul(
                                bk(nb), HSEL[0:16, p * 128:(p + 1) * 128],
                                RECA, start=True, stop=True),
                             reads=["hsel", "reca"], writes=[("psb", nb)])
                        for hh in range(2):
                            h = p + 8 * hh
                            hr = (p % 2) * 64
                            hb = (p // 2 + 4 * hh) * 512
                            rb = hh * 64
                            P.op("dve", lambda v, hr=hr, hb=hb, nb=nb, rb=rb:
                                 v.tensor_mul(
                                    ATT[hr:hr + 64, hb:hb + 512],
                                    ATT[hr:hr + 64, hb:hb + 512],
                                    bk(nb, p0=rb, p1=rb + 64)),
                                 reads=[("psb", nb), ("att", h)],
                                 writes=[("att", h)])
                    for kk in range(ND):
                        if kk < 4:
                            ph = (2 * kk, 2 * kk + 1)
                        else:
                            ph = (2 * (kk - 4) + 8, 2 * (kk - 4) + 9)
                        slot = 6 + (kk % 2)
                        P.dma("qpool", wslot(slot),
                              g["wo"][kk * 128:(kk + 1) * 128, :],
                              writes=[("wb", slot)])
                        for it in range(4):
                            for dh2 in range(2):
                                bank = it * 2 + dh2
                                P.op("pe", lambda t, kk=kk, it=it, dh2=dh2,
                                     bank=bank, slot=slot: t.matmul(
                                        bk(bank),
                                        ATT[:, kk * 512 + it * 128:
                                            kk * 512 + (it + 1) * 128],
                                        wslot(slot)[:, dh2 * 512:(dh2 + 1) * 512],
                                        start=(kk == 0), stop=(kk == ND - 1)),
                                     reads=[("att", ph[0]), ("att", ph[1]),
                                            ("wb", slot)],
                                     writes=[("psb", bank)])
                    for it in range(4):
                        i = qc * 4 + it
                        ln_apply(i, bk(it * 2), bk(it * 2 + 1),
                                 f_reads=[("psb", it * 2), ("psb", it * 2 + 1)])
                    if qc < 3:
                        q_proj(qc + 1, 0, range(ND))
                        q_finish(qc + 1, 0)

        def stage_l1f():
            with ExitStack() as st:
                def lsb(n, s, d=F32):
                    return st.enter_context(nc.sbuf_tensor(n, s, d))

                def lps(n, s, d=F32):
                    return st.enter_context(nc.psum_tensor(n, s, d))
                PS = [lps(f"pm{i}", [128, 512]) for i in range(8)]
                PROBS = lsb("PROBS", [128, NT * 8])
                MASK = lsb("MASK", [128, NT * 8])
                SSE = lsb("SSE", [128, NT * 8])
                SSET = lsb("SSET", [8, NT * 128], BF16)
                MX8 = lsb("MX8", [128, 8])
                KNK = lsb("KNK", [128, 8])
                KEEP = lsb("KEEP", [128, NT])
                CMPB = lsb("CMPB", [128, 8])
                CNT = lsb("CNT", [1, 8])
                CC8 = lsb("CC8", [8, 8])
                CALL = lsb("CALL", [8, 8])
                OH = lsb("OH", [1, 8])
                MLT = lsb("MLT", [8, 128])
                BM = lsb("BM", [8, 1024], BF16)
                WGM = lsb("WGM", [128, 64], BF16)
                ACC = lsb("ACC", [128, 8 * 1024])

                P.dma("qsp", OH[:], g["onehot"], writes=["oh"])
                P.dma("qsp", MLT[:], g["masklt_rep"], writes=["mlt"])
                P.dma("qpool", BM[:], g["bm"], writes=["bm"])
                P.dma("qpool", WGM[:].rearrange("p (kc f) -> p kc f", f=8),
                      col_slab(g["wgm"], 0, 8), writes=["wgm"])
                for g2 in range(2):
                    base = g2 * 4
                    for kc in range(4):
                        r0 = g2 * 512 + kc * 128
                        P.dma("qpool",
                              WB[:, (base + kc) * 1024:
                                 (base + kc + 1) * 1024],
                              g["wet"][0, r0:r0 + 128, :],
                              writes=[("wb", base + kc)])

                for i in range(NT):
                    for k in range(ND):
                        tb_ = 6 + ((i * ND + k) % 2)
                        P.op("pe", lambda t, i=i, k=k, tb_=tb_: t.transpose(
                                PS[tb_][:, 0:128],
                                RES[:, i * D + k * 128: i * D + (k + 1) * 128],
                                IDF[:]),
                             reads=[("res", i), "ident_f"],
                             writes=[("psb", tb_)])
                        P.op("dve", lambda v, i=i, k=k, tb_=tb_: v.tensor_copy(
                                XT[:, k * T + i * 128: k * T + (i + 1) * 128],
                                PS[tb_][:, 0:128]),
                             reads=[("psb", tb_)], writes=[("xt", k, i)])
                    rb_ = 3 * (i % 2)
                    for k in range(ND):
                        P.op("pe", lambda t, k=k, i=i, rb_=rb_: t.matmul(
                                PS[rb_][:, 0:8],
                                XT[:, k * T + i * 128: k * T + (i + 1) * 128],
                                WGM[:, k * 8:(k + 1) * 8],
                                start=(k == 0), stop=(k == ND - 1)),
                             reads=["wgm", *xt_r(i)], writes=[("psb", rb_)])
                    P.op("act", lambda s, i=i, rb_=rb_: s.activation(
                            PROBS[:, i * 8:(i + 1) * 8], PS[rb_][:, 0:8],
                            AF.Exp, accum_out=LNC[:, 0:1]),
                         reads=[("psb", rb_)], writes=["probs", "lnc0"])
                    P.op("dve", lambda v: v.reciprocal(LNC[:, 1:2], LNC[:, 0:1]),
                         reads=["lnc0"], writes=["lnc1"])
                    P.op("dve", lambda v, i=i: v.tensor_scalar(
                            PROBS[:, i * 8:(i + 1) * 8],
                            PROBS[:, i * 8:(i + 1) * 8], LNC[:, 1:2], None,
                            ALU.mult),
                         reads=["probs", "lnc1"], writes=["probs"])
                    P.op("dve", lambda v, i=i: v.max(
                            MX8[:], PROBS[:, i * 8:(i + 1) * 8]),
                         reads=["probs"], writes=["mx8"])
                    P.op("dve", lambda v: v.memset(MX8[:, 1:8], -2.0),
                         writes=["mx8"])
                    P.op("dve", lambda v, i=i: v.match_replace(
                            KNK[:], in_to_replace=MX8[:],
                            in_values=PROBS[:, i * 8:(i + 1) * 8],
                            imm_value=-5.0),
                         reads=["probs", "mx8"], writes=["knk"])
                    P.op("dve", lambda v, i=i: v.tensor_tensor(
                            MASK[:, i * 8:(i + 1) * 8],
                            PROBS[:, i * 8:(i + 1) * 8], KNK[:], ALU.not_equal),
                         reads=["probs", "knk"], writes=[("mask", i)])
                    P.op("dve", lambda v, i=i: v.tensor_mul(
                            SSE[:, i * 8:(i + 1) * 8],
                            MASK[:, i * 8:(i + 1) * 8],
                            PROBS[:, i * 8:(i + 1) * 8]),
                         reads=[("mask", i), "probs"], writes=["sse"])
                    P.op("pe", lambda t, i=i: t.matmul(
                            PS[1][0:1, 0:8], ONE1[:, 0:1],
                            MASK[:, i * 8:(i + 1) * 8],
                            start=(i == 0), stop=(i == NT - 1)),
                         reads=[("mask", i), "one1"], writes=[("psb", 1)])
                    P.op("pe", lambda t, i=i: t.transpose(
                            PS[2][0:8, 0:128], SSE[:, i * 8:(i + 1) * 8], IDF[:]),
                         reads=["sse", "ident_f"], writes=[("psb", 2)])
                    P.op("dve", lambda v, i=i: v.tensor_copy(
                            SSET[0:8, i * 128:(i + 1) * 128], PS[2][0:8, 0:128]),
                         reads=[("psb", 2)], writes=["sset"])

                P.op("dve", lambda v: v.tensor_copy(CNT[:], PS[1][0:1, 0:8]),
                     reads=[("psb", 1)], writes=["cnt"])
                P.op("pe", lambda t: t.matmul(PS[3][0:8, 0:8], OH[:], CNT[:],
                                              start=True, stop=True),
                     reads=["oh", "cnt"], writes=[("psb", 3)])
                P.op("dve", lambda v: v.tensor_copy(CC8[:], PS[3][0:8, 0:8]),
                     reads=[("psb", 3)], writes=["cc8"])
                P.dma("qpool", cc_in_t.ap(), CC8[:], reads=["cc8"],
                      writes=["ccin"])
                P.cc(lambda gp: gp.collective_compute(
                        "AllReduce", ALU.add,
                        replica_groups=[[0, 1, 2, 3, 4, 5, 6, 7]],
                        ins=[cc_in_t.ap().opt()], outs=[cc_out_t.ap().opt()]),
                     reads=["ccin"], writes=["ccout"])
                P.dma("qpool", CALL[:], cc_out_t.ap(), reads=["ccout"],
                      writes=["call"])

                def keep_tile(i):
                    P.op("pe", lambda t, i=i: t.matmul(
                            PS[0][:, 0:8], LSTRF[:],
                            MASK[:, i * 8:(i + 1) * 8], start=True, stop=False),
                         reads=["l_strict", ("mask", i)], writes=[("psb", 0)])
                    for j in range(i):
                        P.op("pe", lambda t, j=j: t.matmul(
                                PS[0][:, 0:8], ONESF[:],
                                MASK[:, j * 8:(j + 1) * 8],
                                start=False, stop=False),
                             reads=["ones_f", ("mask", j)], writes=[("psb", 0)])
                    P.op("pe", lambda t: t.matmul(
                            PS[0][:, 0:8], MLT[:], CALL[0:8, 0:8],
                            start=False, stop=True),
                         reads=["mlt", "call"], writes=[("psb", 0)])
                    P.op("dve", lambda v: v.tensor_single_scalar(
                            CMPB[:], PS[0][:, 0:8], float(CAP) - 0.5, ALU.is_le),
                         reads=[("psb", 0)], writes=["cmpb"])
                    P.op("dve", lambda v, i=i: v.tensor_mul(
                            CMPB[:], CMPB[:], MASK[:, i * 8:(i + 1) * 8]),
                         reads=["cmpb", ("mask", i)], writes=["cmpb"])
                    P.op("dve", lambda v, i=i: v.reduce_sum(
                            KEEP[:, i:i + 1], CMPB[:], AX.X),
                         reads=["cmpb"], writes=[("keep", i)])

                for half in range(2):
                    i0 = half * 8
                    for it in range(8):
                        i = i0 + it
                        b0, b1 = 4 + (it % 2) * 2, 5 + (it % 2) * 2
                        P.op("pe", lambda t, i=i, b0=b0: t.matmul(
                                PS[b0][:, 0:512],
                                SSET[0:8, i * 128:(i + 1) * 128],
                                BM[0:8, 0:512], start=True, stop=True),
                             reads=["sset", "bm"], writes=[("psb", b0)])
                        P.op("pe", lambda t, i=i, b1=b1: t.matmul(
                                PS[b1][:, 0:512],
                                SSET[0:8, i * 128:(i + 1) * 128],
                                BM[0:8, 512:1024], start=True, stop=True),
                             reads=["sset", "bm"], writes=[("psb", b1)])
                        P.op("dve", lambda v, it=it, b0=b0: v.tensor_copy(
                                ACC[:, it * 1024: it * 1024 + 512],
                                PS[b0][:, 0:512]),
                             reads=[("psb", b0)], writes=[("acc", it)])
                        P.op("dve", lambda v, it=it, b1=b1: v.tensor_copy(
                                ACC[:, it * 1024 + 512:(it + 1) * 1024],
                                PS[b1][:, 0:512]),
                             reads=[("psb", b1)], writes=[("acc", it)])
                    for e in range(EXP):
                        if not (half == 0 and e == 0):
                            for g2 in range(2):
                                base = g2 * 4
                                for kc in range(4):
                                    r0 = g2 * 512 + kc * 128
                                    P.dma("qpool",
                                          WB[:, (base + kc) * 1024:
                                             (base + kc + 1) * 1024],
                                          g["wet"][e, r0:r0 + 128, :],
                                          writes=[("wb", base + kc)])
                        for tq4 in range(2):
                            for g2 in range(2):
                                base = g2 * 4
                                for it4 in range(4):
                                    it = tq4 * 4 + it4
                                    i = i0 + it
                                    for dh2 in range(2):
                                        bank = it4 * 2 + dh2
                                        for k4 in range(4):
                                            kk = g2 * 4 + k4
                                            P.op("pe", lambda t, kk=kk, i=i,
                                                 dh2=dh2, bank=bank, base=base,
                                                 k4=k4: t.matmul(
                                                    PS[bank][:, 0:512],
                                                    XT[:, kk * T + i * 128:
                                                       kk * T + (i + 1) * 128],
                                                    WB[:, (base + k4) * 1024 +
                                                       dh2 * 512:
                                                       (base + k4) * 1024 +
                                                       (dh2 + 1) * 512],
                                                    start=(kk == 0),
                                                    stop=(kk == ND - 1)),
                                                 reads=[("wb", base + k4),
                                                        *xt_r()],
                                                 writes=[("psb", bank)])
                                        if g2 == 1:
                                            P.op("dve", lambda v, it=it,
                                                 dh2=dh2, i=i, e=e, bank=bank:
                                                 v.scalar_tensor_tensor(
                                                    ACC[:, it * 1024 + dh2 * 512:
                                                        it * 1024 +
                                                        (dh2 + 1) * 512],
                                                    PS[bank][:, 0:512],
                                                    SSE[:, i * 8 + e:
                                                        i * 8 + e + 1],
                                                    ACC[:, it * 1024 + dh2 * 512:
                                                        it * 1024 +
                                                        (dh2 + 1) * 512],
                                                    ALU.mult, ALU.add),
                                                 reads=[("psb", bank), "sse",
                                                        ("acc", it)],
                                                 writes=[("acc", it)])
                    for it in range(8):
                        i = i0 + it
                        keep_tile(i)
                        ln_apply(i, ACC[:, it * 1024: it * 1024 + 512],
                                 ACC[:, it * 1024 + 512:(it + 1) * 1024],
                                 f_reads=[("acc", it)],
                                 keep_col=KEEP[:, i:i + 1],
                                 keep_reads=[("keep", i)])

        build_xt("s0")
        P.barrier()
        if "l0a" in stages:
            stage_l0a()
            tap("y1", res_r)
            P.barrier()
        if "l0f" in stages:
            stage_l0f()
            tap("y2", res_r)
            P.barrier()
        if "l1a" in stages:
            stage_l1a()
            tap("y3", res_r)
            P.barrier()
        if "l1f" in stages:
            stage_l1f()
            tap("y4", res_r)
            P.barrier()

        for i in range(NT):
            P.dma("qsp", out[i * 128:(i + 1) * 128, :], RES[:, i * D:(i + 1) * D],
                  reads=[("res", i)])
        P.barrier(engines=("sp",))

        block = stack.enter_context(nc.Block())
        P.finish(block)

    return nc


def kernel(**inputs):
    in_maps = make_in_maps(inputs)
    nc = build_program()
    res = run_bass_kernel_spmd(nc, in_maps, core_ids=list(range(8)))
    return np.stack([np.asarray(r["out"], dtype=np.float32)
                     for r in res.results], 0)
